# version 19
# baseline (speedup 1.0000x reference)
import numpy as np
import ml_dtypes
import concourse.bass as bass
import concourse.mybir as mybir
from concourse.bass_utils import run_bass_kernel_spmd
from contextlib import ExitStack

F32 = mybir.dt.float32
BF16 = mybir.dt.bfloat16
AF = mybir.ActivationFunctionType
ALU = mybir.AluOpType

D_MODEL = 2048
M = 1024
MH = 1027
CH = 128
NCH = 8
KD = 16
XN_STRIDE = 1032
NEG = -1e30
EPS = 1e-5


class SemT:
    def __init__(self, nc, es, name):
        self.h = es.enter_context(nc.semaphore(name))
        self.n = 0

    def inc(self, instr, k=1):
        instr.then_inc(self.h, k)
        self.n += k
        return instr

    def dinc(self, instr):
        instr.then_inc(self.h, 16)
        self.n += 16
        return instr

    def wait(self, eng, n=None):
        eng.wait_ge(self.h, self.n if n is None else n)


SERIALIZE = True


class Chain:

    _PASS = ("wait_ge", "engine")

    def __init__(self, eng, sem):
        self._e = eng
        self._s = sem

    def __getattr__(self, name):
        f = getattr(self._e, name)
        if name in self._PASS or not callable(f):
            return f

        def g(*a, **k):
            if SERIALIZE:
                self._e.wait_ge(self._s.h, self._s.n)
            instr = f(*a, **k)
            if SERIALIZE and name != "nop" and instr is not None:
                self._s.inc(instr)
            return instr
        return g


def ap3(t, offset, d1, d2):
    return bass.AP(tensor=t, offset=offset,
                   ap=[[t.ap().ap[0][0], t.ap().ap[0][1]], list(d1), list(d2)])


def build_nc(debug=False):
    nc = bass.Bass("TRN2", target_bir_lowering=False, debug=False)
    DIO = "ExternalOutput" if debug else "Internal"

    def din(name, shape, dt=F32):
        return nc.dram_tensor(name, shape, dt, kind="ExternalInput").ap()

    xtm = din("xtm", [MH, D_MODEL])
    xfmT = din("xfmT", [D_MODEL, M])
    wxt = din("wxt", [48, 128, 2048])
    wzt = din("wzt", [8, 128, 8192])
    wdtt = din("wdtt", [128, 1024])
    wgt = din("wgt", [44, 128, 2048])
    wut = din("wut", [44, 128, 2048])
    wdt2 = din("wdt2", [16, 128, 5632])
    wot = din("wot", [32, 128, 2048])
    convw = din("convw", [128, 192])
    convb = din("convb", [128, 48])
    hrows = din("hrows", [1, 192])
    nwcol = din("nwcol", [128, 32])
    gscol = din("gscol", [128, 44])
    uscol = din("uscol", [128, 44])
    dscol = din("dscol", [128, 16])
    alphas = din("alphas", [128, 4])
    zeros8 = din("zeros8", [128, 8])
    epscol = din("epscol", [128, 1])
    trif = din("trif", [128, 128])
    triuf = din("triuf", [128, 128])
    identf = din("identf", [128, 128])
    identb = din("identb", [128, 128], BF16)
    negtriT = din("negtriT", [128, 128], BF16)
    identx4 = din("identx4", [128, 512], BF16)
    rep2 = din("rep2", [128, 8192], BF16)
    ones2 = din("ones2", [2, 128], BF16)
    onescol = din("onescol", [128, 1])
    onesrow = din("onesrow", [1, 128])

    outT = nc.dram_tensor("outT", [D_MODEL, M], F32, kind="ExternalOutput").ap()
    acs_hi_d = nc.dram_tensor("acs_hi_d", [64, M], BF16, kind=DIO).ap()
    acs_lo_d = nc.dram_tensor("acs_lo_d", [64, M], BF16, kind=DIO).ap()
    states_d = nc.dram_tensor("states_d", [2 * NCH, 128, 2048], F32, kind=DIO).ap()
    xtm_park = nc.dram_tensor("xtm_park", [2, 128, 16384], BF16, kind=DIO).ap()
    b_park = nc.dram_tensor("b_park", [2, 128, 4096], BF16, kind=DIO).ap()
    c_park = nc.dram_tensor("c_park", [2, 128, 4096], BF16, kind=DIO).ap()
    z_park = nc.dram_tensor("z_park", [2, 128, 16384], BF16, kind=DIO).ap()
    g_d = nc.dram_tensor("g_d", [D_MODEL, M], BF16, kind=DIO).ap()
    s1_d = nc.dram_tensor("s1_d", [D_MODEL, M], F32, kind=DIO).ap()
    sin_d = nc.dram_tensor("sin_d", [128, 4096], F32, kind=DIO).ap()
    ssb_d = nc.dram_tensor("ssb_d", [16, 128, 2048], BF16, kind=DIO).ap()
    wxt_b = nc.dram_tensor("wxt_b", [48, 128, 2048], BF16, kind="Internal").ap()
    wzt_b = nc.dram_tensor("wzt_b", [8, 128, 8192], BF16, kind="Internal").ap()
    wdtt_b = nc.dram_tensor("wdtt_b", [128, 1024], BF16, kind="Internal").ap()
    wdt2_b = nc.dram_tensor("wdt2_b", [16, 128, 5632], BF16, kind="Internal").ap()
    wot_b = nc.dram_tensor("wot_b", [32, 128, 2048], BF16, kind="Internal").ap()
    cc_in = nc.dram_tensor("cc_in", [128, 4096], BF16, kind="Internal").ap()
    cc_out = nc.dram_tensor("cc_out", [1024, 4096], BF16, kind="Internal").ap()
    if debug:
        dbg_dt = nc.dram_tensor("dbg_dt", [128, 512], F32, kind="ExternalOutput").ap()
        dbg_ss = nc.dram_tensor("dbg_ss", [128, 4096], F32, kind="ExternalOutput").ap()
        dbg_yfm = nc.dram_tensor("dbg_yfm", [128, 16384], BF16, kind="ExternalOutput").ap()
        dbg_emat = nc.dram_tensor("dbg_emat", [128, 512], F32, kind="ExternalOutput").ap()
        dbg_decay = nc.dram_tensor("dbg_decay", [128, 512], BF16, kind="ExternalOutput").ap()
        dbg_expa = nc.dram_tensor("dbg_expa", [128, 512], F32, kind="ExternalOutput").ap()
        dbg_vst = nc.dram_tensor("dbg_vst", [128, 1024], BF16, kind="ExternalOutput").ap()
        dbg_xn = nc.dram_tensor("dbg_xn", [128, KD * XN_STRIDE], BF16, kind="ExternalOutput").ap()

    es = ExitStack()
    with es:
        def mk(stack):
            def sb(name, shape, dt=F32):
                return stack.enter_context(nc.sbuf_tensor(name, shape, dt))
            return sb

        sb = mk(es)

        def psf(name, shape, dt=F32):
            return es.enter_context(nc.psum_tensor(name, shape, dt))

        s_idb = sb("s_idb", [128, 128], BF16)
        s_ngt = sb("s_ngt", [128, 128], BF16)
        s_idx4 = sb("s_idx4", [128, 512], BF16)
        s_rep2 = sb("s_rep2", [128, 8192], BF16)
        s_ones2 = sb("s_ones2", [2, 128], BF16)
        s_onesc = sb("s_onesc", [128, 1])
        s_onesr = sb("s_onesr", [1, 128])
        s_convw = sb("s_convw", [128, 192])
        s_convb = sb("s_convb", [128, 48])
        s_nw = sb("s_nw", [128, 32])
        s_gs = sb("s_gs", [128, 44])
        s_us = sb("s_us", [128, 44])
        s_ds = sb("s_ds", [128, 16])
        s_alpha = sb("s_alpha", [128, 4])
        s_zero8 = sb("s_zero8", [128, 8])
        s_epsc = sb("s_epsc", [128, 1])
        s_hrows = sb("s_hrows", [1, 192])
        s_red = sb("s_red", [128, 16])
        s_alr = sb("s_alr", [1, 512])
        s_emat = sb("s_emat", [128, 512])
        s_expa = sb("s_expa", [128, 512])
        s_decay = sb("s_decay", [128, 512], BF16)
        s_vst = sb("s_vst", [128, 1024], BF16)
        s_dtb_b = sb("s_dtb_b", [128, 64])
        s_a_b = sb("s_a_b", [128, 64])
        s_d_b = sb("s_d_b", [128, 64])
        s_db16 = sb("s_db16", [128, 64], BF16)
        s_nwb = sb("s_nwb", [128, 32], BF16)
        s_ccb = sb("s_ccb", [128, 4096], BF16)

        pBIG = psf("pBIG", [128, 2048])
        pS1 = psf("pS1", [128, 512])
        pS2 = psf("pS2", [128, 512])
        pT = psf("pT", [128, 1024], BF16)

        L = SemT(nc, es, "L")
        Gd = SemT(nc, es, "Gd")
        P = SemT(nc, es, "P")
        V = SemT(nc, es, "V")
        Ac = SemT(nc, es, "Ac")
        Gp = SemT(nc, es, "Gp")
        St = SemT(nc, es, "St")
        Cc = SemT(nc, es, "Cc")
        Pk = SemT(nc, es, "Pk")
        V2 = SemT(nc, es, "V2")
        A2 = SemT(nc, es, "A2")
        Lx = SemT(nc, es, "Lx")
        Lr = SemT(nc, es, "Lr")
        Lr1 = SemT(nc, es, "Lr1")
        Lrld = SemT(nc, es, "Lrld")
        Lep = SemT(nc, es, "Lep")
        Ls0 = SemT(nc, es, "Ls0")
        Ls1 = SemT(nc, es, "Ls1")
        Gd0 = SemT(nc, es, "Gd0")
        Gd1 = SemT(nc, es, "Gd1")
        Gu0 = SemT(nc, es, "Gu0")
        Gu1 = SemT(nc, es, "Gu1")
        Dd0 = SemT(nc, es, "Dd0")
        Dd1 = SemT(nc, es, "Dd1")
        Sb = SemT(nc, es, "Sb")
        CVS = [SemT(nc, es, f"Cv{i}") for i in range(16)]

        block = es.enter_context(nc.Block())

        def VEC(f):
            @block.vector
            def _(v, __f=f):
                __f(Chain(v, V2))

        def SCA(f):
            @block.scalar
            def _(sc, __f=f):
                __f(Chain(sc, A2))

        def barrier():
            @block.tensor
            def _(t):
                t.wait_ge(P.h, P.n)
                P.inc(t.nop(nofuse=True))

            @block.vector
            def _(v):
                v.wait_ge(V2.h, V2.n)
                V.inc(v.nop(nofuse=True))

            @block.scalar
            def _(sc):
                sc.wait_ge(A2.h, A2.n)
                Ac.inc(sc.nop(nofuse=True))

            @block.gpsimd
            def _(g):
                g.wait_ge(Gp.h, Gp.n)
                Gp.inc(g.nop(nofuse=True))

            for name in ("tensor", "vector", "scalar", "gpsimd", "sync"):
                @getattr(block, name)
                def _(e):
                    P.wait(e); V.wait(e); Ac.wait(e); Gp.wait(e)
                    V2.wait(e); A2.wait(e)
                    L.wait(e); Gd.wait(e); St.wait(e); Pk.wait(e)
                    Lx.wait(e); Lr.wait(e); Lr1.wait(e); Lrld.wait(e); Lep.wait(e)
                    Ls0.wait(e); Ls1.wait(e); Gd0.wait(e); Gd1.wait(e)
                    Gu0.wait(e); Gu1.wait(e); Dd0.wait(e); Dd1.wait(e)
                    Sb.wait(e)

        @block.sync
        def _(s):
            for dst, src in [(s_idb, identb), (s_ngt, negtriT), (s_idx4, identx4),
                             (s_rep2, rep2), (s_ones2, ones2), (s_onesc, onescol),
                             (s_onesr, onesrow), (s_convw, convw), (s_convb, convb),
                             (s_nw, nwcol), (s_gs, gscol), (s_us, uscol),
                             (s_ds, dscol), (s_alpha, alphas), (s_hrows, hrows),
                             (s_zero8, zeros8), (s_epsc, epscol)]:
                L.dinc(s.dma_start(out=dst[:], in_=src))
        n_const = L.n
        cv_done = {}

        def emit_conversions(g, chunks, ci0):
            ci = ci0
            for name, src, dst, rngs in chunks:
                for a, b in rngs:
                    sem = CVS[ci]; ci += 1
                    if name == "wdtt":
                        sem.dinc(g.dma_start(out=dst, in_=src))
                    else:
                        sem.dinc(g.dma_start(out=dst[a:b], in_=src[a:b]))
                    for i in range(a, max(b, a + 1)):
                        cv_done[(name, i)] = (sem, sem.n)
            return ci

        @block.gpsimd
        def _(g):
            g.wait_ge(L.h, n_const)
            emit_conversions(g, [("wdtt", wdtt, wdtt_b, [(0, 0)]),
                                 ("wxt", wxt, wxt_b, [(0, 12)])], 0)
        for _ename in ("tensor", "vector", "scalar", "gpsimd"):
            @getattr(block, _ename)
            def _(e):
                e.wait_ge(L.h, n_const)

        with ExitStack() as scA:
            sbA = mk(scA)
            s_xn = sbA("s_xn", [128, KD * XN_STRIDE], BF16)
            s_ss = sbA("s_ss", [128, 4096])

            with ExitStack() as sc1:
                sb1 = mk(sc1)
                s_xtile = sb1("s_xtile", [128, 2 * D_MODEL])
                pT2 = sc1.enter_context(nc.psum_tensor("pT2a", [128, 1024], BF16))
                s_xnt = sb1("s_xnt", [128, 2 * D_MODEL], BF16)
                s_sqs = sb1("s_sqs", [128, D_MODEL])

                tile_rows = [tt * 128 for tt in range(8)] + [MH - 128]
                ln_ld = {}; ln_v = {}; ln_a = {}; ln_b = {}; ln_ap = {}; ln_tr = {}; ln_ev = {}
                LX = [Lx, Lr]
                for ti, r0 in enumerate(tile_rows):
                    pi = ti % 2

                    @block.sync
                    def _(s, ti=ti, r0=r0, pi=pi):
                        if ti >= 2 and (ti - 2) in ln_a:
                            s.wait_ge(Ac.h, ln_ap[ti - 2])
                        LX[pi].dinc(s.dma_start(out=s_xtile[:, pi * 2048:(pi + 1) * 2048],
                                                in_=xtm[r0:r0 + 128, :]))
                    ln_ld[ti] = (LX[pi], LX[pi].n)

                    @SCA
                    def _(sc, ti=ti, pi=pi):
                        sc.wait_ge(ln_ld[ti][0].h, ln_ld[ti][1])
                        xt = s_xtile[:, pi * 2048:(pi + 1) * 2048]
                        rb = pi * 8
                        sc.activation(s_sqs[:], xt, AF.Square, accum_out=s_red[:, rb + 1:rb + 2])
                        Ac.inc(sc.nop(nofuse=True))
                    ln_sq = Ac.n

                    @VEC
                    def _(v, ti=ti, pi=pi, ln_sq=ln_sq):
                        v.wait_ge(ln_ld[ti][0].h, ln_ld[ti][1])
                        v.wait_ge(Ac.h, ln_sq)
                        xt = s_xtile[:, pi * 2048:(pi + 1) * 2048]
                        rb = pi * 8
                        v.reduce_sum(s_red[:, rb:rb + 1], xt, axis=mybir.AxisListType.X)
                        v.tensor_scalar_mul(s_red[:, rb + 2:rb + 3], s_red[:, rb:rb + 1], 1.0 / D_MODEL)
                        v.tensor_mul(s_red[:, rb + 3:rb + 4], s_red[:, rb + 2:rb + 3], s_red[:, rb + 2:rb + 3])
                        v.scalar_tensor_tensor(s_red[:, rb + 4:rb + 5], s_red[:, rb + 1:rb + 2], 1.0 / D_MODEL,
                                               s_red[:, rb + 3:rb + 4], ALU.mult, ALU.subtract)
                        V.inc(v.nop(nofuse=True))
                    ln_v[ti] = V.n

                    @SCA
                    def _(sc, ti=ti, pi=pi):
                        sc.wait_ge(V.h, ln_v[ti])
                        rb = pi * 8
                        sc.activation(s_red[:, rb + 5:rb + 6], s_red[:, rb + 4:rb + 5], AF.Ln,
                                      bias=s_epsc[:, 0:1])
                        sc.activation(s_red[:, rb + 5:rb + 6], s_red[:, rb + 5:rb + 6], AF.Exp, scale=-0.5)
                        Ac.inc(sc.nop(nofuse=True))
                    ln_a[ti] = Ac.n

                    @VEC
                    def _(v, ti=ti, pi=pi):
                        v.wait_ge(Ac.h, ln_a[ti])
                        rb = pi * 8
                        v.scalar_tensor_tensor(s_red[:, rb + 6:rb + 7], s_red[:, rb + 2:rb + 3], -1.0,
                                               s_red[:, rb + 5:rb + 6], ALU.mult, ALU.mult)
                        V.inc(v.nop(nofuse=True))
                    ln_b[ti] = V.n

                    @SCA
                    def _(sc, ti=ti, pi=pi):
                        sc.wait_ge(V.h, ln_b[ti])
                        if ti >= 2 and (ti - 2) in ln_tr:
                            sc.wait_ge(P.h, ln_tr[ti - 2])
                        rb = pi * 8
                        sc.activation(s_xnt[:, pi * 2048:(pi + 1) * 2048],
                                      s_xtile[:, pi * 2048:(pi + 1) * 2048], AF.Identity,
                                      scale=s_red[:, rb + 5:rb + 6], bias=s_red[:, rb + 6:rb + 7])
                        Ac.inc(sc.nop(nofuse=True))
                    ln_ap[ti] = Ac.n

                    for grp in range(4):
                        pTx = pT if grp % 2 == 0 else pT2

                        @block.tensor
                        def _(t, ti=ti, grp=grp, pi=pi, pTx=pTx):
                            t.wait_ge(Ac.h, ln_ap[ti])
                            key = (ti, grp - 2) if grp >= 2 else (ti - 1, grp + 2)
                            if key in ln_ev:
                                t.wait_ge(V.h, ln_ev[key])
                            for j in range(4):
                                dtile = grp * 4 + j
                                mm = t.transpose(pTx[:, j * 128:(j + 1) * 128],
                                                 s_xnt[:, pi * 2048 + dtile * 128: pi * 2048 + (dtile + 1) * 128],
                                                 s_idb[:])
                            P.inc(mm)
                        ln_tr[(ti, grp)] = P.n

                        @VEC
                        def _(v, ti=ti, grp=grp, r0=r0, pTx=pTx):
                            v.wait_ge(P.h, ln_tr[(ti, grp)])
                            dst = ap3(s_xn, (grp * 4) * XN_STRIDE + r0, (XN_STRIDE, 4), (1, 128))
                            v.tensor_copy(dst, pTx.ap()[:, 0:512].rearrange(
                                "p (j c) -> p j c", j=4))
                            V.inc(v.nop(nofuse=True))
                        ln_ev[(ti, grp)] = V.n
                    ln_tr[ti] = P.n

                @VEC
                def _(v):
                    V.wait(v)
                    pad = ap3(s_xn, 1027, (XN_STRIDE, KD), (1, XN_STRIDE - 1027))
                    src = ap3(s_zero8, 0, (0, KD), (1, XN_STRIDE - 1027))
                    v.tensor_copy(pad, src)
                    V.inc(v.nop(nofuse=True))

            barrier()

            @block.gpsimd
            def _(g):
                emit_conversions(g, [("wxt", wxt, wxt_b,
                                      [(12, 24), (24, 36), (36, 48)])], 2)

            with ExitStack() as sc2:
                sb2 = mk(sc2)
                s_tri = sb2("s_tri", [128, 128])
                s_triu = sb2("s_triu", [128, 128])
                s_idf = sb2("s_idf", [128, 128])
                s_dt = sb2("s_dt", [128, 512])
                s_lndt = sb2("s_lndt", [128, 512])
                s_adt = sb2("s_adt", [128, 512])
                s_vtm = sb2("s_vtm", [128, 512])
                s_acsfm = sb2("s_acsfm", [64, 1024])
                s_acshi = sb2("s_acshi", [64, 1024], BF16)
                s_acslo = sb2("s_acslo", [64, 1024], BF16)
                s_vfm = sb2("s_vfm", [64, 128])
                s_wdt = sb2("s_wdt", [128, KD * 64], BF16)
                pROW = sc2.enter_context(nc.psum_tensor("pROW", [1, 512], F32))

                @block.sync
                def _(s):
                    for dst, src in [(s_tri, trif), (s_triu, triuf), (s_idf, identf)]:
                        L.dinc(s.dma_start(out=dst[:], in_=src))

                @block.scalar
                def _(sc):
                    sc.wait_ge(cv_done[("wdtt", 0)][0].h, cv_done[("wdtt", 0)][1])
                    Gd.dinc(sc.dma_start(out=s_wdt[:], in_=wdtt_b))

                @block.tensor
                def _(t):
                    L.wait(t)
                    t.matmul(pS1[:, 0:64], s_onesr[:], s_hrows[:, 0:64], start=True, stop=True)
                    t.matmul(pS1[:, 64:128], s_onesr[:], s_hrows[:, 64:128], start=True, stop=True)
                    mm = t.matmul(pS1[:, 128:192], s_onesr[:], s_hrows[:, 128:192], start=True, stop=True)
                    P.inc(mm)

                @VEC
                def _(v):
                    P.wait(v)
                    v.tensor_copy(s_dtb_b[:], pS1[:, 0:64])
                    v.tensor_copy(s_a_b[:], pS1[:, 64:128])
                    v.tensor_copy(s_d_b[:], pS1[:, 128:192])
                    v.tensor_copy(s_db16[:], pS1[:, 128:192])
                    v.tensor_copy(s_nwb[:], s_nw[:])
                    V.inc(v.nop(nofuse=True))

                @block.tensor
                def _(t):
                    Gd.wait(t)
                    V.wait(t)
                    for tt in range(8):
                        for k in range(KD):
                            lhsT = s_xn[:, k * XN_STRIDE + 3 + tt * 128: k * XN_STRIDE + 3 + tt * 128 + 128]
                            mm = t.matmul(pBIG[:, tt * 64:(tt + 1) * 64], lhsT,
                                          s_wdt[:, k * 64:(k + 1) * 64],
                                          start=(k == 0), stop=(k == KD - 1))
                    P.inc(mm)

                @VEC
                def _(v):
                    P.wait(v)
                    db_ap = ap3(s_dtb_b, 0, (0, 8), (1, 64))
                    v.tensor_tensor(s_dt.ap().rearrange("p (c h) -> p c h", c=8),
                                    pBIG.ap()[:, 0:512].rearrange("p (c h) -> p c h", c=8),
                                    db_ap, ALU.add)
                    v.tensor_scalar_mul(s_vtm[:], s_dt[:], -1.0)
                    v.tensor_max(s_vtm[:], s_vtm[:], s_dt[:])
                    V.inc(v.nop(nofuse=True))

                @SCA
                def _(sc):
                    V.wait(sc)
                    sc.activation(s_lndt[:], s_vtm[:], AF.Exp, scale=-1.0)
                    sc.activation(s_lndt[:], s_lndt[:], AF.Ln, bias=1.0)
                    Ac.inc(sc.nop(nofuse=True))

                @VEC
                def _(v):
                    Ac.wait(v)
                    v.tensor_scalar_max(s_adt[:], s_dt[:], 0.0)
                    v.tensor_add(s_dt[:], s_adt[:], s_lndt[:])
                    V.inc(v.nop(nofuse=True))

                @SCA
                def _(sc):
                    V.wait(sc)
                    sc.activation(s_lndt[:], s_dt[:], AF.Ln)
                    Ac.inc(sc.nop(nofuse=True))

                @VEC
                def _(v):
                    Ac.wait(v)
                    a_ap = ap3(s_a_b, 0, (0, 8), (1, 64))
                    v.tensor_tensor(s_adt.ap().rearrange("p (c h) -> p c h", c=8),
                                    s_dt.ap().rearrange("p (c h) -> p c h", c=8),
                                    a_ap, ALU.mult)
                    V.inc(v.nop(nofuse=True))

                @block.tensor
                def _(t):
                    V.wait(t)
                    for c in range(NCH):
                        adt_c = s_adt[:, c * 64:(c + 1) * 64]
                        lndt_c = s_lndt[:, c * 64:(c + 1) * 64]
                        t.matmul(pBIG[:, 512 + c * 64: 512 + (c + 1) * 64], s_tri[:], adt_c,
                                 start=True, stop=True)
                        t.matmul(pROW[:, c * 64:(c + 1) * 64], s_onesc[:], adt_c,
                                 start=True, stop=True)
                        t.matmul(pBIG[0:64, 1024 + c * 128: 1024 + (c + 1) * 128], adt_c, s_tri[:],
                                 start=True, stop=True)
                        t.matmul(pS2[:, c * 64:(c + 1) * 64], s_triu[:], adt_c,
                                 start=True, stop=False)
                        mm = t.matmul(pS2[:, c * 64:(c + 1) * 64], s_idf[:], lndt_c,
                                      start=False, stop=True)
                    P.inc(mm)

                @VEC
                def _(v):
                    P.wait(v)
                    v.tensor_sub(s_vtm[:], s_lndt[:], pBIG[:, 512:1024])
                    v.tensor_copy(s_acsfm[:], pBIG[0:64, 1024:2048])
                    v.tensor_copy(s_alr[:], pROW[:])
                    V.inc(v.nop(nofuse=True))

                @SCA
                def _(sc):
                    V.wait(sc)
                    sc.activation(s_expa[:], pBIG[:, 512:1024], AF.Exp)
                    sc.activation(s_decay[:], pS2[:], AF.Exp)
                    sc.activation(s_alr[:], s_alr[:], AF.Exp)
                    Ac.inc(sc.nop(nofuse=True))

                @block.tensor
                def _(t):
                    Ac.wait(t)
                    mm = t.matmul(pBIG[:, 0:512], s_onesr[:], s_alr[:], start=True, stop=True)
                    P.inc(mm)
                emat_p = P.n

                @block.tensor
                def _(t):
                    V.wait(t)
                    for c in range(NCH):
                        mm = t.transpose(pBIG[0:64, 1024 + c * 128: 1024 + (c + 1) * 128],
                                         s_vtm[:, c * 64:(c + 1) * 64], s_idf[:])
                    P.inc(mm)

                @VEC
                def _(v):
                    P.wait(v)
                    v.tensor_copy(s_emat[:], pBIG[:, 0:512])
                    v.tensor_copy(s_acshi[:], s_acsfm[:])
                    v.scalar_tensor_tensor(s_acslo[:], s_acshi[:], -1.0, s_acsfm[:],
                                           ALU.mult, ALU.add)
                    v.tensor_copy(s_vst[0:64, :], pBIG[0:64, 1024:2048])
                    v.scalar_tensor_tensor(s_vst[64:128, :], s_vst[0:64, :], -1.0,
                                           pBIG[0:64, 1024:2048], ALU.mult, ALU.add)
                    V.inc(v.nop(nofuse=True))

                @block.sync
                def _(s):
                    V.wait(s)
                    Pk.dinc(s.dma_start(out=acs_hi_d, in_=s_acshi[:]))
                    Pk.dinc(s.dma_start(out=acs_lo_d, in_=s_acslo[:]))
                    if debug:
                        St.dinc(s.dma_start(out=dbg_dt, in_=s_dt[:]))
                        St.dinc(s.dma_start(out=dbg_emat, in_=s_emat[:]))
                        St.dinc(s.dma_start(out=dbg_decay, in_=s_decay[:]))
                        St.dinc(s.dma_start(out=dbg_expa, in_=s_expa[:]))
                        St.dinc(s.dma_start(out=dbg_vst, in_=s_vst[:]))

            barrier()

            with ExitStack() as sc3:
                sb3 = mk(sc3)
                s_xtm = sb3("s_xtm", [128, 16384], BF16)
                s_bfm = sb3("s_bfm", [128, 4096], BF16)
                s_cfm = sb3("s_cfm", [128, 4096], BF16)
                s_btm = sb3("s_btm", [128, 4096], BF16)
                s_xr = sb3("s_xr", [128, 2 * 1032])
                s_cv = sb3("s_cv", [128, 2 * 1024])
                s_cu = sb3("s_cu", [128, 1024])
                s_xf = sb3("s_xf", [128, 2 * 1024], BF16)
                s_xdec = sb3("s_xdec", [128, 2048], BF16)
                s_stc = sb3("s_stc", [128, 2048])
                s_cu2 = sb3("s_cu2", [128, 1024])
                s_wx = sb3("s_wx", [128, 2 * 2048], BF16)
                pT2 = sc3.enter_context(nc.psum_tensor("pT2b", [128, 1024], BF16))

                xslot_free = [0, 0]
                xgd_done = {}
                for hp in range(2):
                    if hp == 1:
                        vh0 = V.n

                        @block.gpsimd
                        def _(g, vh0=vh0):
                            g.wait_ge(V.h, vh0)
                            emit_conversions(g, [("wzt", wzt, wzt_b, [(0, 4), (4, 8)])], 5)
                    tiles = []
                    for xi in range(16):
                        tiles.append(((2048 * hp + xi * 128) // 128, "x", xi))
                    for bi in range(4):
                        tiles.append(((4096 + 512 * hp + bi * 128) // 128, "b", bi))
                    for ci in range(4):
                        tiles.append(((5120 + 512 * hp + ci * 128) // 128, "c", ci))

                    xb_mm = {}; xb_ev = {}; xb_cv = {}; xb_si = {}; xb_tr = {}; xb_xe = {}
                    for tix, (ct, kind, idx) in enumerate(tiles):
                        slot = tix % 2
                        if slot == 0:
                            pdst = [pBIG[:, 0:343], pBIG[:, 512:855], pBIG[:, 1024:1367]]
                        else:
                            pdst = [pBIG[:, 1536:1879], pS1[:, 0:343], pS2[:, 0:343]]
                        xro = slot * 1032
                        cvo = slot * 1024
                        xfo = slot * 1024

                        @block.scalar
                        def _(sc, ct=ct, slot=slot):
                            sc.wait_ge(cv_done[("wxt", ct)][0].h, cv_done[("wxt", ct)][1])
                            sc.wait_ge(P.h, xslot_free[slot])
                            (Gd0 if slot == 0 else Gd1).dinc(sc.dma_start(
                                out=s_wx[:, slot * 2048:(slot + 1) * 2048],
                                in_=wxt_b[ct]))
                        xgd_done[(hp, tix)] = (Gd0 if slot == 0 else Gd1, (Gd0 if slot == 0 else Gd1).n)

                        @block.tensor
                        def _(t, slot=slot, hp=hp, tix=tix, pdst=pdst):
                            t.wait_ge(xgd_done[(hp, tix)][0].h, xgd_done[(hp, tix)][1])
                            if tix >= 2:
                                t.wait_ge(V.h, xb_ev[tix - 2])
                            elif hp == 1:
                                V.wait(t)
                            for tc in range(3):
                                for k in range(KD):
                                    lhsT = s_wx[:, slot * 2048 + k * 128: slot * 2048 + (k + 1) * 128]
                                    rhs = s_xn[:, k * XN_STRIDE + tc * 343: k * XN_STRIDE + (tc + 1) * 343]
                                    mm = t.matmul(pdst[tc], lhsT, rhs,
                                                  start=(k == 0), stop=(k == KD - 1))
                            P.inc(mm)
                        xslot_free[slot] = P.n
                        xb_mm[tix] = P.n

                        @VEC
                        def _(v, tix=tix, slot=slot, xro=xro, ct=ct, cvo=cvo):
                            v.wait_ge(P.h, xb_mm[tix])
                            if tix >= 2:
                                v.wait_ge(Ac.h, xb_si[tix - 2])
                            if slot == 0:
                                src = bass.AP(tensor=pBIG, offset=0,
                                              ap=[[pBIG.ap().ap[0][0], 128], [512, 3], [1, 343]])
                                v.tensor_copy(
                                    s_xr.ap()[:, xro:xro + 1029].rearrange("p (t c) -> p t c", t=3), src)
                            else:
                                v.tensor_copy(s_xr[:, xro:xro + 343], pBIG[:, 1536:1879])
                                v.tensor_copy(s_xr[:, xro + 343:xro + 686], pS1[:, 0:343])
                                v.tensor_copy(s_xr[:, xro + 686:xro + 1029], pS2[:, 0:343])
                            v.tensor_scalar_mul(s_cv[:, cvo:cvo + 1024], s_xr[:, xro:xro + 1024],
                                                s_convw[:, ct * 4: ct * 4 + 1])
                            for kk in range(1, 4):
                                v.scalar_tensor_tensor(s_cv[:, cvo:cvo + 1024],
                                                       s_xr[:, xro + kk:xro + kk + 1024],
                                                       s_convw[:, ct * 4 + kk: ct * 4 + kk + 1],
                                                       s_cv[:, cvo:cvo + 1024], ALU.mult, ALU.add)
                            V.inc(v.nop(nofuse=True))
                        xb_ev[tix] = V.n
                        xb_cv[tix] = V.n

                        @SCA
                        def _(sc, tix=tix, ct=ct, kind=kind, idx=idx, cvo=cvo, xfo=xfo):
                            sc.wait_ge(V.h, xb_cv[tix])
                            if kind == "x":
                                if tix >= 2 and (tix - 2) in xb_tr:
                                    sc.wait_ge(P.h, xb_tr[tix - 2])
                                dst = s_xf[:, xfo:xfo + 1024]
                            elif kind == "b":
                                dst = s_bfm[:, idx * 1024:(idx + 1) * 1024]
                            else:
                                dst = s_cfm[:, idx * 1024:(idx + 1) * 1024]
                            sc.activation(dst, s_cv[:, cvo:cvo + 1024], AF.Silu,
                                          bias=s_convb[:, ct:ct + 1])
                            Ac.inc(sc.nop(nofuse=True))
                        xb_si[tix] = Ac.n

                        if kind == "x":
                            for half in range(2):
                                pTx = pT if half % 2 == 0 else pT2

                                @block.tensor
                                def _(t, tix=tix, half=half, xfo=xfo, pTx=pTx):
                                    t.wait_ge(Ac.h, xb_si[tix])
                                    key = (tix - 2, half)
                                    if key in xb_xe:
                                        t.wait_ge(V.h, xb_xe[key])
                                    for j in range(4):
                                        c = half * 4 + j
                                        mm = t.transpose(pTx[:, j * 128:(j + 1) * 128],
                                                         s_xf[:, xfo + c * 128: xfo + (c + 1) * 128],
                                                         s_idb[:])
                                    P.inc(mm)
                                xb_tr[(tix, half)] = P.n

                                @VEC
                                def _(v, tix=tix, half=half, idx=idx, pTx=pTx):
                                    v.wait_ge(P.h, xb_tr[(tix, half)])
                                    dst = ap3(s_xtm, (half * 4) * 2048 + idx * 128, (2048, 4), (1, 128))
                                    v.tensor_copy(dst, pTx.ap()[:, 0:512].rearrange(
                                        "p (j c) -> p j c", j=4))
                                    V.inc(v.nop(nofuse=True))
                                xb_xe[(tix, half)] = V.n
                            xb_tr[tix] = P.n

                    for bi in range(4):
                        for half in range(2):
                            @block.tensor
                            def _(t, bi=bi, half=half):
                                Ac.wait(t)
                                V.wait(t)
                                pTx = pT if half % 2 == 0 else pT2
                                for j in range(4):
                                    c = half * 4 + j
                                    mm = t.transpose(pTx[:, j * 128:(j + 1) * 128],
                                                     s_bfm[:, bi * 1024 + c * 128: bi * 1024 + (c + 1) * 128],
                                                     s_idb[:])
                                P.inc(mm)

                            @VEC
                            def _(v, bi=bi, half=half):
                                P.wait(v)
                                pTx = pT if half % 2 == 0 else pT2
                                dst = ap3(s_btm, (half * 4) * 512 + bi * 128, (512, 4), (1, 128))
                                v.tensor_copy(dst, pTx.ap()[:, 0:512].rearrange("p (j c) -> p j c", j=4))
                                V.inc(v.nop(nofuse=True))

                    for c in range(NCH):
                        @VEC
                        def _(v, c=c, hp=hp):
                            V.wait(v)
                            Pk.wait(v)
                            dec = ap3(s_decay, c * 64 + 32 * hp, (1, 32), (0, 64))
                            v.tensor_tensor(
                                s_xdec.ap().rearrange("p (h q) -> p h q", h=32),
                                ap3(s_xtm, c * 2048, (64, 32), (1, 64)),
                                dec, ALU.mult)
                            V.inc(v.nop(nofuse=True))
                        st_xd = V.n

                        @block.tensor
                        def _(t, c=c, st_xd=st_xd):
                            t.wait_ge(V.h, st_xd)
                            pds = [pS1.ap(), pS2.ap(), pBIG[:, 0:512], pBIG[:, 512:1024]]
                            for gl in range(4):
                                ps = pds[gl]
                                for hh in range(8):
                                    mm = t.matmul(ps[:, hh * 64:(hh + 1) * 64],
                                                  s_btm[:, c * 512 + gl * 128: c * 512 + (gl + 1) * 128],
                                                  s_xdec[:, (gl * 8 + hh) * 64:(gl * 8 + hh + 1) * 64],
                                                  start=True, stop=True)
                            P.inc(mm)
                        st_mm = P.n

                        @VEC
                        def _(v, c=c, hp=hp, st_mm=st_mm):
                            v.wait_ge(P.h, st_mm)
                            v.tensor_copy(s_stc[:, 0:512], pS1[:, 0:512])
                            v.tensor_copy(s_stc[:, 512:1024], pS2[:, 0:512])
                            v.tensor_copy(s_stc[:, 1024:2048], pBIG[:, 0:1024])
                            sl = slice(2048 * hp, 2048 * hp + 2048)
                            if c == 0:
                                v.tensor_copy(s_ss[:, sl], s_stc[:])
                            else:
                                em = ap3(s_emat, c * 64 + 32 * hp, (1, 32), (0, 64))
                                v.tensor_tensor(
                                    s_ss.ap()[:, sl].rearrange("p (h q) -> p h q", h=32),
                                    s_ss.ap()[:, sl].rearrange("p (h q) -> p h q", h=32),
                                    em, ALU.mult)
                                v.tensor_add(s_ss[:, sl], s_ss[:, sl], s_stc[:])
                            V.inc(v.nop(nofuse=True))
                        st_ev = V.n

                        @block.sync
                        def _(s, c=c, hp=hp, st_ev=st_ev):
                            s.wait_ge(V.h, st_ev)
                            Pk.dinc(s.dma_start(out=states_d[hp * NCH + c], in_=s_stc[:]))

                    @block.sync
                    def _(s, hp=hp):
                        V.wait(s)
                        Ac.wait(s)
                        P.wait(s, P.n)
                        Pk.dinc(s.dma_start(out=xtm_park[hp], in_=s_xtm[:]))
                        Pk.dinc(s.dma_start(out=b_park[hp], in_=s_bfm[:]))
                        Pk.dinc(s.dma_start(out=c_park[hp], in_=s_cfm[:]))

                    @VEC
                    def _(v):
                        Pk.wait(v)
                        V.inc(v.nop(nofuse=True))

            vh36 = V.n

            @block.gpsimd
            def _(g, vh36=vh36):
                g.wait_ge(V.h, vh36)
                emit_conversions(g, [("wdt2", wdt2, wdt2_b, [(0, 8), (8, 16)])], 7)

            @VEC
            def _(v):
                v.tensor_copy(s_ccb[:], s_ss[:])
                V.inc(v.nop(nofuse=True))

            @block.sync
            def _(s):
                V.wait(s)
                L.dinc(s.dma_start(out=cc_in, in_=s_ccb[:]))
                if debug:
                    St.dinc(s.dma_start(out=dbg_ss, in_=s_ss[:]))

            @block.gpsimd
            def _(g):
                L.wait(g)
                Cc.inc(g.collective_compute(
                    "AllGather", ALU.bypass,
                    replica_groups=[list(range(8))],
                    ins=[cc_in.opt()], outs=[cc_out.opt()],
                ))

            barrier()

            with ExitStack() as scz:
                sbz = mk(scz)
                s_wz = sbz("s_wz", [128, 2 * 8192], BF16)
                s_z = sbz("s_z", [128, 16384], BF16)

                def emit_s9s95():
                    with ExitStack() as sc9:
                        sb9 = mk(sc9)
                        s_sin = sb9("s_sin", [128, 4096])

                        for r in range(4):
                            @block.sync
                            def _(s, r=r):
                                if r == 0:
                                    Cc.wait(s, 1)
                                else:
                                    V.wait(s)
                                L.dinc(s.dma_start(out=s_ccb[:],
                                                   in_=cc_out[r * 256:r * 256 + 128, :]))

                            @VEC
                            def _(v, r=r):
                                L.wait(v)
                                if r == 0:
                                    v.tensor_scalar_mul(s_sin[:], s_ccb[:], s_alpha[:, 0:1])
                                else:
                                    v.scalar_tensor_tensor(s_sin[:], s_ccb[:], s_alpha[:, r:r + 1],
                                                           s_sin[:], ALU.mult, ALU.add)
                                V.inc(v.nop(nofuse=True))

                        @block.sync
                        def _(s):
                            V.wait(s)
                            Sb.dinc(s.dma_start(out=sin_d, in_=s_sin[:]))

                    with ExitStack() as sc95:
                        sb95 = mk(sc95)
                        s_pss = sb95("s_pss", [128, 2048])
                        s_pst = sb95("s_pst", [128, 2 * 2048])
                        s_pssb = sb95("s_pssb", [128, 2048], BF16)

                        for hp in range(2):
                            @block.sync
                            def _(s, hp=hp):
                                if hp == 0:
                                    Sb.wait(s)
                                else:
                                    V.wait(s)
                                Lrld.dinc(s.dma_start(out=s_pss[:],
                                                      in_=sin_d[:, 2048 * hp: 2048 * hp + 2048]))
                                Lrld.dinc(s.dma_start(out=ap3(s_pst, 0, (2048, 1), (1, 2048)),
                                                      in_=states_d[hp * NCH + 0]))

                            for c in range(NCH):
                                @block.sync
                                def _(s, c=c, hp=hp):
                                    if c >= 1:
                                        V.wait(s)
                                    if c + 1 < NCH:
                                        Lrld.dinc(s.dma_start(
                                            out=ap3(s_pst, ((c + 1) % 2) * 2048, (2048, 1), (1, 2048)),
                                            in_=states_d[hp * NCH + c + 1]))

                                @VEC
                                def _(v, c=c, hp=hp):
                                    Lrld.wait(v)
                                    Sb.wait(v)
                                    v.tensor_copy(s_pssb[:], s_pss[:])
                                    em = ap3(s_emat, c * 64 + 32 * hp, (1, 32), (0, 64))
                                    v.tensor_tensor(s_pss.ap().rearrange("p (h q) -> p h q", h=32),
                                                    s_pss.ap().rearrange("p (h q) -> p h q", h=32),
                                                    em, ALU.mult)
                                    v.tensor_add(s_pss[:], s_pss[:],
                                                 s_pst[:, (c % 2) * 2048:((c % 2) + 1) * 2048])
                                    V.inc(v.nop(nofuse=True))

                                @block.sync
                                def _(s, c=c, hp=hp):
                                    V.wait(s)
                                    Sb.dinc(s.dma_start(out=ssb_d[hp * NCH + c], in_=s_pssb[:]))

                zslot_free = [0, 0]
                zgd_done = {}
                z_ac = {}
                for hp in range(2):
                    for zc in range(4):
                        slot = (hp * 4 + zc) % 2
                        six = hp * 4 + zc

                        @block.scalar
                        def _(sc, six=six, slot=slot):
                            sc.wait_ge(cv_done[("wzt", six)][0].h, cv_done[("wzt", six)][1])
                            sc.wait_ge(P.h, zslot_free[slot])
                            (Gd0 if slot == 0 else Gd1).dinc(sc.dma_start(
                                out=s_wz[:, slot * 8192:(slot + 1) * 8192],
                                in_=wzt_b[six]))
                        zgd_done[six] = (Gd0 if slot == 0 else Gd1, (Gd0 if slot == 0 else Gd1).n)

                        for tt in range(8):
                            zseq = six * 8 + tt

                            @block.tensor
                            def _(t, tt=tt, slot=slot, six=six, zseq=zseq):
                                t.wait_ge(zgd_done[six][0].h, zgd_done[six][1])
                                if zseq >= 2:
                                    t.wait_ge(Ac.h, z_ac[zseq - 2])
                                ps = pS1 if tt % 2 == 0 else pS2
                                for k in range(KD):
                                    lhsT = s_xn[:, k * XN_STRIDE + 3 + tt * 128: k * XN_STRIDE + 3 + tt * 128 + 128]
                                    mm = t.matmul(ps[:, 0:512], lhsT,
                                                  s_wz[:, slot * 8192 + k * 512: slot * 8192 + (k + 1) * 512],
                                                  start=(k == 0), stop=(k == KD - 1))
                                P.inc(mm)

                            @SCA
                            def _(sc, zc=zc, tt=tt):
                                P.wait(sc)
                                ps = pS1 if tt % 2 == 0 else pS2
                                sc.activation(
                                    s_z[:, tt * 2048 + zc * 512: tt * 2048 + (zc + 1) * 512],
                                    ps[:, 0:512], AF.Silu)
                                Ac.inc(sc.nop(nofuse=True))
                            z_ac[zseq] = Ac.n

                        zslot_free[slot] = P.n

                    @block.sync
                    def _(s, hp=hp):
                        Ac.wait(s)
                        Pk.dinc(s.dma_start(out=z_park[hp], in_=s_z[:]))

                    @SCA
                    def _(sc):
                        Pk.wait(sc)
                        Ac.inc(sc.nop(nofuse=True))

                    if hp == 0:
                        ah_z0 = Ac.n

                        @block.gpsimd
                        def _(g, ah_z0=ah_z0):
                            g.wait_ge(Ac.h, ah_z0)
                            emit_conversions(g, [("wot", wot, wot_b, [(0, 16), (16, 32)])], 9)

                        emit_s9s95()

            barrier()

            with ExitStack() as sc8:
                sb8 = mk(sc8)
                s_wg = sb8("s_wg", [128, 3 * 2048], BF16)
                s_wu = sb8("s_wu", [128, 3 * 2048], BF16)
                s_wd = sb8("s_wd", [128, 2 * 5632], BF16)
                s_wgf = sb8("s_wgf", [128, 2 * 2048])
                s_hdn = sb8("s_hdn", [128, 44 * 512], BF16)
                s_sg = sb8("s_sg", [128, 1024])
                s_epg = sb8("s_epg", [128, 512], BF16)

                g3_free = [0, 0, 0]
                uslot_free = [0, 0, 0]
                gcast_done = {}
                dslot_free = [0, 0]
                dgd_done = {}
                ffn_pg = {}; ffn_ac = {}; ffn_v = {}
                down_ac = {}
                pend_stt = None
                for tc in range(2):
                    for it in range(44):
                        slot = it % 2
                        j = tc * 44 + it
                        gs = j % 3
                        us = j % 3

                        @block.scalar
                        def _(sc, it=it, slot=slot, j=j):
                            if j >= 2:
                                sc.wait_ge(V.h, gcast_done[j - 2])
                            semg = Gd0 if slot == 0 else Gd1
                            semg.dinc(sc.dma_start(
                                out=s_wgf[:, slot * 2048:(slot + 1) * 2048], in_=wgt[it]))
                        gld_g = ((Gd0 if slot == 0 else Gd1), (Gd0 if slot == 0 else Gd1).n)

                        @block.gpsimd
                        def _(g, it=it, slot=slot, j=j, us=us):
                            g.wait_ge(P.h, uslot_free[us])
                            semu = Gu0 if slot == 0 else Gu1
                            semu.dinc(g.dma_start(
                                out=s_wu[:, us * 2048:(us + 1) * 2048], in_=wut[it]))
                        gld_u = ((Gu0 if slot == 0 else Gu1), (Gu0 if slot == 0 else Gu1).n)

                        @VEC
                        def _(v, gs=gs, slot=slot, j=j, gld_g=gld_g):
                            v.wait_ge(gld_g[0].h, gld_g[1])
                            v.wait_ge(P.h, g3_free[gs])
                            v.tensor_copy(s_wg[:, gs * 2048:(gs + 1) * 2048],
                                          s_wgf[:, slot * 2048:(slot + 1) * 2048])
                            V.inc(v.nop(nofuse=True))
                        gcast_done[j] = V.n

                        if pend_stt is not None:
                            pend_stt()
                        pend_stt = None

                        pg = pBIG[:, 0:512] if j % 2 == 0 else pS1[:, 0:512]
                        pu = pBIG[:, 512:1024] if j % 2 == 0 else pS2[:, 0:512]

                        @block.tensor
                        def _(t, tc=tc, slot=slot, gs=gs, us=us, it=it, j=j, pg=pg, pu=pu,
                              gld_u=gld_u):
                            t.wait_ge(V.h, gcast_done[j])
                            t.wait_ge(gld_u[0].h, gld_u[1])
                            if j >= 2:
                                t.wait_ge(Ac.h, ffn_ac[j - 2])
                                t.wait_ge(V.h, ffn_v[j - 2])
                            for k in range(KD):
                                rhs = s_xn[:, k * XN_STRIDE + 3 + tc * 512: k * XN_STRIDE + 3 + (tc + 1) * 512]
                                mm = t.matmul(pg,
                                              s_wg[:, gs * 2048 + k * 128: gs * 2048 + (k + 1) * 128],
                                              rhs, start=(k == 0), stop=(k == KD - 1))
                            P.inc(mm)
                            ffn_pg[j] = P.n
                            for k in range(KD):
                                rhs = s_xn[:, k * XN_STRIDE + 3 + tc * 512: k * XN_STRIDE + 3 + (tc + 1) * 512]
                                mm = t.matmul(pu,
                                              s_wu[:, us * 2048 + k * 128: us * 2048 + (k + 1) * 128],
                                              rhs, start=(k == 0), stop=(k == KD - 1))
                            P.inc(mm)
                        g3_free[gs] = P.n
                        uslot_free[us] = P.n
                        ffn_pu = P.n

                        @SCA
                        def _(sc, it=it, j=j, pg=pg):
                            sc.wait_ge(P.h, ffn_pg[j])
                            sc.activation(s_sg[:, (j % 2) * 512:(j % 2) * 512 + 512],
                                          pg, AF.Silu, scale=s_gs[:, it:it + 1])
                            Ac.inc(sc.nop(nofuse=True))
                        ffn_ac[j] = Ac.n

                        def _mk_stt(it=it, j=j, pu=pu, _pun=ffn_pu):
                            @VEC
                            def _(v):
                                v.wait_ge(P.h, _pun)
                                v.wait_ge(Ac.h, ffn_ac[j])
                                v.scalar_tensor_tensor(s_hdn[:, it * 512:(it + 1) * 512],
                                                       pu, s_us[:, it:it + 1],
                                                       s_sg[:, (j % 2) * 512:(j % 2) * 512 + 512],
                                                       ALU.mult, ALU.mult)
                                V.inc(v.nop(nofuse=True))
                            ffn_v[j] = V.n
                        pend_stt = _mk_stt

                    if pend_stt is not None:
                        pend_stt()
                    pend_stt = None

                    for dt_ in range(16):
                        slot = dt_ % 2

                        @block.scalar
                        def _(sc, dt_=dt_, slot=slot):
                            sc.wait_ge(cv_done[("wdt2", dt_)][0].h, cv_done[("wdt2", dt_)][1])
                            sc.wait_ge(P.h, dslot_free[slot])
                            (Dd0 if slot == 0 else Dd1).dinc(sc.dma_start(
                                out=s_wd[:, slot * 5632:(slot + 1) * 5632], in_=wdt2_b[dt_]))
                        dgd_done[(tc, dt_)] = (Dd0 if slot == 0 else Dd1, (Dd0 if slot == 0 else Dd1).n)

                        @block.tensor
                        def _(t, tc=tc, dt_=dt_, slot=slot):
                            t.wait_ge(dgd_done[(tc, dt_)][0].h, dgd_done[(tc, dt_)][1])
                            if dt_ == 0:
                                V.wait(t)
                            if tc * 16 + dt_ >= 2:
                                t.wait_ge(Ac.h, down_ac[tc * 16 + dt_ - 2])
                            elif dt_ == 0:
                                Ac.wait(t)
                            ps = pS1 if dt_ % 2 == 0 else pS2
                            for k in range(44):
                                mm = t.matmul(ps[:, 0:512],
                                              s_wd[:, slot * 5632 + k * 128: slot * 5632 + (k + 1) * 128],
                                              s_hdn[:, k * 512:(k + 1) * 512],
                                              start=(k == 0), stop=(k == 43))
                            P.inc(mm)
                        dslot_free[slot] = P.n

                        @SCA
                        def _(sc, dt_=dt_):
                            P.wait(sc)
                            Pk.wait(sc)
                            ps = pS1 if dt_ % 2 == 0 else pS2
                            sc.activation(s_epg[:], ps[:, 0:512], AF.Silu,
                                                 scale=s_ds[:, dt_:dt_ + 1])
                            Ac.inc(sc.nop(nofuse=True))
                        down_ac[tc * 16 + dt_] = Ac.n

                        @block.sync
                        def _(s, dt_=dt_, tc=tc):
                            Ac.wait(s)
                            Pk.dinc(s.dma_start(out=g_d[dt_ * 128:(dt_ + 1) * 128,
                                                        tc * 512:(tc + 1) * 512],
                                                in_=s_epg[:]))

            barrier()

        barrier()

        with ExitStack() as scB:
            sbB = mk(scB)
            s_yfm = sbB("s_yfm", [128, 16384], BF16)

            with ExitStack() as scy:
                sby = mk(scy)
                s_xtc = sby("s_xtc", [128, 2 * 2048], BF16)
                s_zc = sby("s_zc", [128, 2 * 2048], BF16)
                s_sbc = sby("s_sbc", [128, 2 * 2048], BF16)
                s_bfm = sby("s_bfm2", [128, 4096], BF16)
                s_cfm = sby("s_cfm2", [128, 4096], BF16)
                s_rows = sby("s_rows", [2, 2 * 1024], BF16)
                s_E = sby("s_E", [128, 2 * 1024], BF16)
                s_G = sby("s_G", [128, 2 * 1024], BF16)
                s_ysc = sby("s_ysc", [128, 512])
                s_ytm = sby("s_ytm", [128, 2048], BF16)
                s_xD = sby("s_xD", [128, 2048], BF16)
                s_ynb = sby("s_ynb", [128, 2048], BF16)
                s_sqd = sby("s_sqd", [128, 512])
                s_ssq = sby("s_ssq", [128, 4])
                s_rg = sby("s_rg", [128, 4])
                s_wo = sby("s_wo", [128, 2 * 2048], BF16)
                s_ep1 = sby("s_ep1", [128, 512])
                s_ep2 = sby("s_ep2", [128, 512])
                s_eps1 = sby("s_eps1", [128, 512])
                s_epg = sby("s_epg2", [128, 512], BF16)
                s_epx = sby("s_epx", [128, 512])
                s_out = sby("s_out", [128, 2 * 512])
                pY = es.enter_context(nc.psum_tensor("pY", [128, 512], F32))

                oslot_free = [0, 0]
                ogd_done = {}
                r_args = {}
                r_E = {}
                r_G = {}
                r_G0 = {}
                r_Ymm = {}
                r_EV = {}
                for hp in range(2):
                    if hp == 1:
                        barrier()

                    @block.sync
                    def _(s, hp=hp):
                        V.wait(s)
                        Ac.wait(s)
                        Lrld.dinc(s.dma_start(out=s_bfm[:], in_=b_park[hp]))
                        Lrld.dinc(s.dma_start(out=s_cfm[:], in_=c_park[hp]))
                        for pc in range(2):
                            Lsp = Ls0 if pc % 2 == 0 else Ls1
                            Lsp.dinc(s.dma_start(out=ap3(s_sbc, pc * 2048, (2048, 1), (1, 2048)),
                                                 in_=ssb_d[hp * NCH + pc]))
                            Lsp.dinc(s.dma_start(out=ap3(s_xtc, pc * 2048, (2048, 1), (1, 2048)),
                                                 in_=xtm_park[hp][:, pc * 2048:(pc + 1) * 2048]))
                            Lsp.dinc(s.dma_start(out=ap3(s_zc, pc * 2048, (2048, 1), (1, 2048)),
                                                 in_=z_park[hp][:, pc * 2048:(pc + 1) * 2048]))
                    hp_rld = Lrld.n

                    def emit_front(c, g):
                        cs = (c % 2) * 2048
                        gp = g % 2
                        h0 = hp * 32 + g * 8

                        @block.sync
                        def _(s, c=c, g=g, gp=gp, h0=h0):
                            key = (hp, c, g - 2) if g >= 2 else (hp, c - 1, g + 2)
                            if key in r_args:
                                s.wait_ge(P.h, r_args[key])
                            Lrs = Lr if gp == 0 else Lr1
                            Lrs.dinc(s.dma_start(
                                out=s_rows[0:1, gp * 1024:(gp + 1) * 1024].rearrange(
                                    "o (a b) -> o a b", a=8),
                                in_=acs_hi_d[h0:h0 + 8, c * 128:(c + 1) * 128].rearrange(
                                    "(o a) b -> o a b", o=1)))
                            Lrs.dinc(s.dma_start(
                                out=s_rows[1:2, gp * 1024:(gp + 1) * 1024].rearrange(
                                    "o (a b) -> o a b", a=8),
                                in_=acs_lo_d[h0:h0 + 8, c * 128:(c + 1) * 128].rearrange(
                                    "(o a) b -> o a b", o=1)))
                        lr_sem = Lr if gp == 0 else Lr1
                        lr_n = lr_sem.n

                        @block.tensor
                        def _(t, c=c, g=g, gp=gp, h0=h0, lr_n=lr_n, lr_sem=lr_sem):
                            t.wait_ge(lr_sem.h, lr_n)
                            key = (hp, c, g - 2) if g >= 2 else (hp, c - 1, g + 2)
                            if key in r_E:
                                t.wait_ge(Ac.h, r_E[key])
                            for nchk in range(2):
                                rr = s_rep2[:, (h0 + nchk * 4) * 128: (h0 + nchk * 4 + 4) * 128]
                                t.matmul(pBIG[:, gp * 1024 + nchk * 512: gp * 1024 + (nchk + 1) * 512],
                                         s_vst[:, c * 128:(c + 1) * 128], rr,
                                         start=True, stop=False)
                                t.matmul(pBIG[:, gp * 1024 + nchk * 512: gp * 1024 + (nchk + 1) * 512],
                                         s_ones2[:],
                                         s_rows[:, gp * 1024 + nchk * 512: gp * 1024 + (nchk + 1) * 512],
                                         start=False, stop=False)
                                mm = t.matmul(pBIG[:, gp * 1024 + nchk * 512: gp * 1024 + (nchk + 1) * 512],
                                              s_ngt[:], s_idx4[:],
                                              start=False, stop=True)
                            P.inc(mm)
                        r_args[(hp, c, g)] = P.n

                        @SCA
                        def _(sc, c=c, g=g, gp=gp):
                            sc.wait_ge(P.h, r_args[(hp, c, g)])
                            key = (hp, c, g - 2) if g >= 2 else (hp, c - 1, g + 2)
                            if key in r_G:
                                sc.wait_ge(V.h, r_G[key])
                            sc.activation(s_E[:, gp * 1024:(gp + 1) * 1024],
                                          pBIG[:, gp * 1024:(gp + 1) * 1024], AF.Exp)
                            Ac.inc(sc.nop(nofuse=True))
                        r_E[(hp, c, g)] = Ac.n

                        @block.tensor
                        def _(t, c=c, g=g):
                            t.wait_ge(Lrld.h, hp_rld)
                            key = (hp, c, g - 1) if g >= 1 else (hp, c - 1, 3)
                            if key in r_G:
                                t.wait_ge(V.h, r_G[key])
                            mm = t.matmul(pS1[:, 0:128],
                                          s_bfm[:, g * 1024 + c * 128: g * 1024 + (c + 1) * 128],
                                          s_cfm[:, g * 1024 + c * 128: g * 1024 + (c + 1) * 128],
                                          start=True, stop=True)
                            P.inc(mm)
                        r_G0[(hp, c, g)] = P.n

                        @VEC
                        def _(v, c=c, g=g, gp=gp):
                            v.wait_ge(P.h, r_G0[(hp, c, g)])
                            v.wait_ge(Ac.h, r_E[(hp, c, g)])
                            g0rep = bass.AP(tensor=pS1, offset=0,
                                            ap=[[pS1.ap().ap[0][0], 128], [0, 8], [1, 128]])
                            v.tensor_tensor(
                                s_G.ap()[:, gp * 1024:(gp + 1) * 1024].rearrange(
                                    "p (h l) -> p h l", h=8),
                                g0rep,
                                s_E.ap()[:, gp * 1024:(gp + 1) * 1024].rearrange(
                                    "p (h l) -> p h l", h=8),
                                ALU.mult)
                            V.inc(v.nop(nofuse=True))
                        r_G[(hp, c, g)] = V.n

                    def emit_back(c, g):
                        cs = (c % 2) * 2048
                        gp = g % 2

                        @block.tensor
                        def _(t, c=c, g=g, gp=gp, cs=cs):
                            t.wait_ge(V.h, r_G[(hp, c, g)])
                            key = (hp, c, g - 1) if g >= 1 else (hp, c - 1, 3)
                            if key in r_EV:
                                t.wait_ge(V.h, r_EV[key])
                            (Ls0 if c % 2 == 0 else Ls1).wait(t)
                            for hh in range(8):
                                hloc = g * 8 + hh
                                t.matmul(pS2[:, hh * 64:(hh + 1) * 64],
                                         s_G[:, gp * 1024 + hh * 128: gp * 1024 + (hh + 1) * 128],
                                         s_xtc[:, cs + hloc * 64: cs + (hloc + 1) * 64],
                                         start=True, stop=True)
                            for hh in range(8):
                                hloc = g * 8 + hh
                                mm = t.matmul(pY[:, hh * 64:(hh + 1) * 64],
                                              s_cfm[:, g * 1024 + c * 128: g * 1024 + (c + 1) * 128],
                                              s_sbc[:, cs + hloc * 64: cs + (hloc + 1) * 64],
                                              start=True, stop=True)
                            P.inc(mm)
                        r_Ymm[(hp, c, g)] = P.n

                        @VEC
                        def _(v, c=c, g=g):
                            v.wait_ge(P.h, r_Ymm[(hp, c, g)])
                            ea = ap3(s_expa, c * 64 + hp * 32 + g * 8, (1, 8), (0, 64))
                            v.tensor_tensor(
                                s_ysc.ap().rearrange("p (h q) -> p h q", h=8),
                                bass.AP(tensor=pY, offset=0,
                                        ap=[[pY.ap().ap[0][0], 128], [64, 8], [1, 64]]),
                                ea, ALU.mult)
                            v.scalar_tensor_tensor(
                                s_ytm[:, g * 512:(g + 1) * 512], s_ysc[:], 1.0,
                                pS2[:, 0:512], ALU.bypass, ALU.add)
                            V.inc(v.nop(nofuse=True))
                        r_EV[(hp, c, g)] = V.n

                    def emit_tail(c):
                        cs = (c % 2) * 2048

                        @VEC
                        def _(v, c=c, cs=cs):
                            dap = ap3(s_db16, hp * 32, (1, 32), (0, 64))
                            v.tensor_tensor(
                                s_xD.ap().rearrange("p (h q) -> p h q", h=32),
                                ap3(s_xtc, cs, (64, 32), (1, 64)),
                                dap, ALU.mult)
                            v.tensor_add(s_ytm[:], s_ytm[:], s_xD[:])
                            v.tensor_mul(s_ytm[:], s_ytm[:], s_zc[:, cs:cs + 2048])
                            V.inc(v.nop(nofuse=True))
                        r_tail = V.n

                        @block.sync
                        def _(s, c=c, cs=cs, r_tail=r_tail):
                            s.wait_ge(V.h, r_tail)
                            if c + 2 < NCH:
                                Lsp = Ls0 if c % 2 == 0 else Ls1
                                Lsp.dinc(s.dma_start(out=ap3(s_sbc, cs, (2048, 1), (1, 2048)),
                                                     in_=ssb_d[hp * NCH + c + 2]))
                                Lsp.dinc(s.dma_start(out=ap3(s_xtc, cs, (2048, 1), (1, 2048)),
                                                     in_=xtm_park[hp][:, (c + 2) * 2048:(c + 3) * 2048]))
                                Lsp.dinc(s.dma_start(out=ap3(s_zc, cs, (2048, 1), (1, 2048)),
                                                     in_=z_park[hp][:, (c + 2) * 2048:(c + 3) * 2048]))

                        @SCA
                        def _(sc, c=c, r_tail=r_tail):
                            sc.wait_ge(V.h, r_tail)
                            for gl in range(4):
                                sc.activation(s_sqd[:], s_ytm[:, gl * 512:(gl + 1) * 512],
                                              AF.Square, accum_out=s_ssq[:, gl:gl + 1])
                            Ac.inc(sc.nop(nofuse=True))

                        @VEC
                        def _(v):
                            Ac.wait(v)
                            v.tensor_scalar_mul(s_rg[:], s_ssq[:], 1.0 / 512.0)
                            V.inc(v.nop(nofuse=True))

                        @SCA
                        def _(sc):
                            V.wait(sc)
                            sc.activation(s_rg[:], s_rg[:], AF.Ln, bias=s_epsc[:, 0:1])
                            sc.activation(s_rg[:], s_rg[:], AF.Exp, scale=-0.5)
                            Ac.inc(sc.nop(nofuse=True))

                        @VEC
                        def _(v):
                            Ac.wait(v)
                            rap = ap3(s_rg, 0, (1, 4), (0, 512))
                            v.tensor_tensor(
                                s_ynb.ap().rearrange("p (g q) -> p g q", g=4),
                                s_ytm.ap().rearrange("p (g q) -> p g q", g=4), rap, ALU.mult)
                            V.inc(v.nop(nofuse=True))

                        for grp in range(4):
                            @block.tensor
                            def _(t, grp=grp):
                                V.wait(t)
                                for j in range(4):
                                    i = grp * 4 + j
                                    mm = t.transpose(pT[:, j * 128:(j + 1) * 128],
                                                     s_ynb[:, i * 128:(i + 1) * 128], s_idb[:])
                                P.inc(mm)

                            @VEC
                            def _(v, grp=grp, c=c):
                                P.wait(v)
                                nwap = ap3(s_nwb, hp * 16 + grp * 4, (1, 4), (0, 128))
                                dst = ap3(s_yfm, (grp * 4) * 1024 + c * 128, (1024, 4), (1, 128))
                                v.tensor_tensor(dst,
                                                pT.ap()[:, 0:512].rearrange("p (j c) -> p j c", j=4),
                                                nwap, ALU.mult)
                                V.inc(v.nop(nofuse=True))

                    stages = [(c, g) for c in range(NCH) for g in range(4)]
                    for si in range(len(stages) + 1):
                        if si < len(stages):
                            emit_front(*stages[si])
                        if si >= 1:
                            cb, gb = stages[si - 1]
                            emit_back(cb, gb)
                            if gb == 3:
                                emit_tail(cb)

                    if debug and hp == 1:
                        @block.sync
                        def _(s):
                            V.wait(s)
                            St.dinc(s.dma_start(out=dbg_yfm, in_=s_yfm[:]))

                    wo_ev = {}
                    for dt_ in range(16):
                        slot = dt_ % 2

                        @block.scalar
                        def _(sc, dt_=dt_, hp=hp, slot=slot):
                            sc.wait_ge(cv_done[("wot", hp * 16 + dt_)][0].h,
                                       cv_done[("wot", hp * 16 + dt_)][1])
                            sc.wait_ge(P.h, oslot_free[slot])
                            (Gd0 if slot == 0 else Gd1).dinc(sc.dma_start(
                                out=s_wo[:, slot * 2048:(slot + 1) * 2048],
                                in_=wot_b[hp * 16 + dt_]))
                        ogd_done[(hp, dt_)] = (Gd0 if slot == 0 else Gd1,
                                               (Gd0 if slot == 0 else Gd1).n)

                        @block.tensor
                        def _(t, dt_=dt_, hp=hp, slot=slot):
                            t.wait_ge(ogd_done[(hp, dt_)][0].h, ogd_done[(hp, dt_)][1])
                            if dt_ == 0:
                                V.wait(t)
                            j = hp * 16 + dt_
                            if j >= 2 and (hp, dt_ - 2) in wo_ev:
                                t.wait_ge(V.h, wo_ev[(hp, dt_ - 2)])
                            elif hp == 1 and dt_ < 2:
                                V.wait(t)
                            pa = pS1[:, 0:512] if dt_ % 2 == 0 else pY[:, 0:512]
                            pb = pS2[:, 0:512] if dt_ % 2 == 0 else pBIG[:, 0:512]
                            for k in range(KD):
                                t.matmul(pa,
                                         s_wo[:, slot * 2048 + k * 128: slot * 2048 + (k + 1) * 128],
                                         s_yfm[:, k * 1024 + 0 * 512: k * 1024 + 1 * 512],
                                         start=(k == 0), stop=(k == KD - 1))
                                mm = t.matmul(pb,
                                              s_wo[:, slot * 2048 + k * 128: slot * 2048 + (k + 1) * 128],
                                              s_yfm[:, k * 1024 + 1 * 512: k * 1024 + 2 * 512],
                                              start=(k == 0), stop=(k == KD - 1))
                            P.inc(mm)
                        oslot_free[slot] = P.n
                        r_wo = P.n

                        for tc in range(2):
                            if hp == 0:
                                @VEC
                                def _(v, dt_=dt_, tc=tc, r_wo=r_wo):
                                    v.wait_ge(P.h, r_wo)
                                    Pk.wait(v)
                                    ps = (pS1[:, 0:512] if dt_ % 2 == 0 else pY[:, 0:512]) if tc == 0 \
                                        else (pS2[:, 0:512] if dt_ % 2 == 0 else pBIG[:, 0:512])
                                    v.tensor_copy(s_ep1[:], ps)
                                    V.inc(v.nop(nofuse=True))

                                @block.sync
                                def _(s, dt_=dt_, tc=tc):
                                    V.wait(s)
                                    Pk.dinc(s.dma_start(out=s1_d[dt_ * 128:(dt_ + 1) * 128,
                                                                 tc * 512:(tc + 1) * 512],
                                                        in_=s_ep1[:]))
                            else:
                                @block.sync
                                def _(s, dt_=dt_, tc=tc):
                                    V.wait(s)
                                    Pk.wait(s)
                                    Lep.dinc(s.dma_start(out=s_eps1[:],
                                                       in_=s1_d[dt_ * 128:(dt_ + 1) * 128,
                                                                tc * 512:(tc + 1) * 512]))
                                    Lep.dinc(s.dma_start(out=s_epg[:],
                                                       in_=g_d[dt_ * 128:(dt_ + 1) * 128,
                                                               tc * 512:(tc + 1) * 512]))
                                    Lep.dinc(s.dma_start(out=s_epx[:],
                                                       in_=xfmT[dt_ * 128:(dt_ + 1) * 128,
                                                                tc * 512:(tc + 1) * 512]))

                                @VEC
                                def _(v, dt_=dt_, tc=tc, r_wo=r_wo):
                                    v.wait_ge(P.h, r_wo)
                                    Lep.wait(v)
                                    St.wait(v)
                                    ps = (pS1[:, 0:512] if dt_ % 2 == 0 else pY[:, 0:512]) if tc == 0 \
                                        else (pS2[:, 0:512] if dt_ % 2 == 0 else pBIG[:, 0:512])
                                    o = ((dt_ * 2 + tc) % 2) * 512
                                    v.scalar_tensor_tensor(s_ep1[:], s_eps1[:], 1.0, ps,
                                                           ALU.bypass, ALU.add)
                                    v.tensor_mul(s_ep2[:], s_ep1[:], s_epg[:])
                                    v.tensor_add(s_out[:, o:o + 512], s_ep2[:], s_epx[:])
                                    V.inc(v.nop(nofuse=True))

                                @block.sync
                                def _(s, dt_=dt_, tc=tc):
                                    V.wait(s)
                                    o = ((dt_ * 2 + tc) % 2) * 512
                                    St.dinc(s.dma_start(
                                        out=outT[dt_ * 128:(dt_ + 1) * 128, tc * 512:(tc + 1) * 512],
                                        in_=s_out[:, o:o + 512]))
                        wo_ev[(hp, dt_)] = V.n

        @block.sync
        def _(s):
            St.wait(s)
            Pk.wait(s)

    return nc


def _prep_consts():
    idx = np.arange(128)
    tri = (idx[:, None] <= idx[None, :]).astype(np.float32)
    triu = (idx[:, None] > idx[None, :]).astype(np.float32)
    ident = np.eye(128, dtype=np.float32)
    identb = ident.astype(ml_dtypes.bfloat16)
    negtriT = np.where(idx[None, :] > idx[:, None], np.float32(NEG),
                       np.float32(0.0)).astype(ml_dtypes.bfloat16)
    identx4 = np.tile(np.eye(128, dtype=np.float32), (1, 4)).astype(ml_dtypes.bfloat16)
    k = np.arange(128)
    hl = np.arange(8192)
    rep2 = ((k[:, None] % 64) == (hl[None, :] // 128)).astype(np.float32).astype(ml_dtypes.bfloat16)
    ones2 = np.ones((2, 128), np.float32).astype(ml_dtypes.bfloat16)
    return dict(trif=tri, triuf=triu, identf=ident, identb=identb, negtriT=negtriT,
                identx4=identx4, rep2=rep2, ones2=ones2,
                onescol=np.ones((128, 1), np.float32),
                onesrow=np.ones((1, 128), np.float32),
                zeros8=np.zeros((128, 8), np.float32),
                epscol=np.full((128, 1), EPS, np.float32))


def make_in_maps(inputs):
    x = np.asarray(inputs["L0_attn_hidden_states"], np.float32)
    W_in = np.asarray(inputs["L0_attn_W_in"], np.float32)
    conv_w = np.asarray(inputs["L0_attn_conv_w"], np.float32)
    conv_b = np.asarray(inputs["L0_attn_conv_b"], np.float32)
    A = np.asarray(inputs["L0_attn_A"], np.float32)
    D = np.asarray(inputs["L0_attn_D"], np.float32)
    dtb = np.asarray(inputs["L0_attn_dt_bias"], np.float32)
    nw = np.asarray(inputs["L0_attn_norm_weight"], np.float32)
    W_out = np.asarray(inputs["L0_attn_W_out"], np.float32)
    gw = np.asarray(inputs["L0_ffn_gate_w"], np.float32)
    uw = np.asarray(inputs["L0_ffn_up_w"], np.float32)
    dw = np.asarray(inputs["L0_ffn_down_w"], np.float32)
    gs = np.asarray(inputs["L0_ffn_gate_s"], np.float32)
    us = np.asarray(inputs["L0_ffn_up_s"], np.float32)
    ds = np.asarray(inputs["L0_ffn_down_s"], np.float32)

    winT = W_in.T
    def tile_cols(wt, c0, c1, kk, w):
        blk = wt[:, c0:c1]
        nt = (c1 - c0) // w
        out = np.empty((nt, 128, kk * w), np.float32)
        for i in range(nt):
            out[i] = blk[:, i * w:(i + 1) * w].reshape(kk, 128, w).transpose(1, 0, 2).reshape(128, kk * w)
        return np.ascontiguousarray(out)

    wxt_np = np.concatenate([tile_cols(winT, 4096, 8192, 16, 128),
                             tile_cols(winT, 8192, 9216, 16, 128),
                             tile_cols(winT, 9216, 10240, 16, 128)], 0)
    idx = []
    for hp in range(2):
        idx += [16 * hp + i for i in range(16)]
        idx += [32 + 4 * hp + i for i in range(4)]
        idx += [40 + 4 * hp + i for i in range(4)]
    wzt_np = tile_cols(winT, 0, 4096, 16, 512)
    wdtt_np = tile_cols(winT, 10240, 10304, 16, 64)[0]
    wgt_np = tile_cols(gw.T, 0, 5632, 16, 128)
    wut_np = tile_cols(uw.T, 0, 5632, 16, 128)
    wdt2_np = tile_cols(dw.T, 0, 2048, 44, 128)
    woT = W_out.T
    wot_np = np.empty((32, 128, 2048), np.float32)
    for hp in range(2):
        for d in range(16):
            blk = woT[2048 * hp:2048 * (hp + 1), d * 128:(d + 1) * 128]
            wot_np[hp * 16 + d] = blk.reshape(16, 128, 128).transpose(1, 0, 2).reshape(128, 2048)

    shared = dict(
        wxt=wxt_np, wzt=wzt_np, wdtt=wdtt_np, wgt=wgt_np, wut=wut_np,
        wdt2=wdt2_np, wot=wot_np,
        convw=np.ascontiguousarray(conv_w.reshape(48, 128, 4).transpose(1, 0, 2).reshape(128, 192)),
        convb=np.ascontiguousarray(conv_b.reshape(48, 128).T),
        hrows=np.concatenate([dtb, A, D]).reshape(1, 192).astype(np.float32),
        nwcol=np.ascontiguousarray(nw.reshape(32, 128).T),
        gscol=np.ascontiguousarray(gs.reshape(44, 128).T),
        uscol=np.ascontiguousarray(us.reshape(44, 128).T),
        dscol=np.ascontiguousarray(ds.reshape(16, 128).T),
        **_prep_consts(),
    )
    maps = []
    for c in range(8):
        b, h = c // 2, c % 2
        t0 = h * 1024
        halo = x[b, t0 - 3:t0] if h == 1 else np.zeros((3, D_MODEL), np.float32)
        alph = np.zeros((128, 4), np.float32)
        if h == 1:
            alph[:, b] = 1.0
        m = dict(shared)
        m["xtm"] = np.ascontiguousarray(np.concatenate([halo, x[b, t0:t0 + 1024]], 0))
        m["xfmT"] = np.ascontiguousarray(x[b, t0:t0 + 1024].T)
        m["alphas"] = alph
        maps.append(m)
    return maps


def kernel(**inputs):
    maps = make_in_maps(inputs)
    nc = build_nc(debug=False)
    res = run_bass_kernel_spmd(nc, maps, core_ids=list(range(8)))
    out = np.zeros((4, 2048, 2048), np.float32)
    for c in range(8):
        b, h = c // 2, c % 2
        out[b, h * 1024:(h + 1) * 1024, :] = res.results[c]["outT"].T
    return out


# version 20 (final)
# speedup vs baseline: 1.0838x; 1.0838x over previous
import numpy as np
import ml_dtypes
import concourse.bass as bass
import concourse.mybir as mybir
from concourse.bass_utils import run_bass_kernel_spmd
from contextlib import ExitStack

F32 = mybir.dt.float32
BF16 = mybir.dt.bfloat16
AF = mybir.ActivationFunctionType
ALU = mybir.AluOpType

D_MODEL = 2048
M = 1024
MH = 1027
CH = 128
NCH = 8
KD = 16
XN_STRIDE = 1032
NEG = -1e30
EPS = 1e-5


class SemT:
    def __init__(self, nc, es, name):
        self.h = es.enter_context(nc.semaphore(name))
        self.n = 0

    def inc(self, instr, k=1):
        instr.then_inc(self.h, k)
        self.n += k
        return instr

    def dinc(self, instr):
        instr.then_inc(self.h, 16)
        self.n += 16
        return instr

    def wait(self, eng, n=None):
        eng.wait_ge(self.h, self.n if n is None else n)


SERIALIZE = True


class Chain:

    _PASS = ("wait_ge", "engine")

    def __init__(self, eng, sem):
        self._e = eng
        self._s = sem

    def __getattr__(self, name):
        f = getattr(self._e, name)
        if name in self._PASS or not callable(f):
            return f

        def g(*a, **k):
            if SERIALIZE:
                self._e.wait_ge(self._s.h, self._s.n)
            instr = f(*a, **k)
            if SERIALIZE and name != "nop" and instr is not None:
                self._s.inc(instr)
            return instr
        return g


def ap3(t, offset, d1, d2):
    return bass.AP(tensor=t, offset=offset,
                   ap=[[t.ap().ap[0][0], t.ap().ap[0][1]], list(d1), list(d2)])


def build_nc(debug=False):
    nc = bass.Bass("TRN2", target_bir_lowering=False, debug=False)
    DIO = "ExternalOutput" if debug else "Internal"

    def din(name, shape, dt=F32):
        return nc.dram_tensor(name, shape, dt, kind="ExternalInput").ap()

    xtm = din("xtm", [MH, D_MODEL])
    xfmT = din("xfmT", [D_MODEL, M])
    wxt = din("wxt", [48, 128, 2048])
    wzt = din("wzt", [8, 128, 8192])
    wdtt = din("wdtt", [128, 1024])
    wgt = din("wgt", [44, 128, 2048])
    wut = din("wut", [44, 128, 2048])
    wdt2 = din("wdt2", [16, 128, 5632])
    wot = din("wot", [32, 128, 2048])
    convw = din("convw", [128, 192])
    convb = din("convb", [128, 48])
    hrows = din("hrows", [1, 192])
    nwcol = din("nwcol", [128, 32])
    gscol = din("gscol", [128, 44])
    uscol = din("uscol", [128, 44])
    dscol = din("dscol", [128, 16])
    alphas = din("alphas", [128, 4])
    zeros8 = din("zeros8", [128, 8])
    epscol = din("epscol", [128, 1])
    trif = din("trif", [128, 128])
    triuf = din("triuf", [128, 128])
    identf = din("identf", [128, 128])
    identb = din("identb", [128, 128], BF16)
    negtriT = din("negtriT", [128, 128], BF16)
    identx4 = din("identx4", [128, 512], BF16)
    rep2 = din("rep2", [128, 8192], BF16)
    ones2 = din("ones2", [2, 128], BF16)
    onescol = din("onescol", [128, 1])
    onesrow = din("onesrow", [1, 128])

    outT = nc.dram_tensor("outT", [D_MODEL, M], F32, kind="ExternalOutput").ap()
    acs_hi_d = nc.dram_tensor("acs_hi_d", [64, M], BF16, kind=DIO).ap()
    acs_lo_d = nc.dram_tensor("acs_lo_d", [64, M], BF16, kind=DIO).ap()
    states_d = nc.dram_tensor("states_d", [2 * NCH, 128, 2048], F32, kind=DIO).ap()
    xtm_park = nc.dram_tensor("xtm_park", [2, 128, 16384], BF16, kind=DIO).ap()
    b_park = nc.dram_tensor("b_park", [2, 128, 4096], BF16, kind=DIO).ap()
    c_park = nc.dram_tensor("c_park", [2, 128, 4096], BF16, kind=DIO).ap()
    z_park = nc.dram_tensor("z_park", [2, 128, 16384], BF16, kind=DIO).ap()
    g_d = nc.dram_tensor("g_d", [D_MODEL, M], BF16, kind=DIO).ap()
    s1_d = nc.dram_tensor("s1_d", [D_MODEL, M], F32, kind=DIO).ap()
    sin_d = nc.dram_tensor("sin_d", [128, 4096], F32, kind=DIO).ap()
    ssb_d = nc.dram_tensor("ssb_d", [16, 128, 2048], BF16, kind=DIO).ap()
    wxt_b = nc.dram_tensor("wxt_b", [48, 128, 2048], BF16, kind="Internal").ap()
    wzt_b = nc.dram_tensor("wzt_b", [8, 128, 8192], BF16, kind="Internal").ap()
    wdtt_b = nc.dram_tensor("wdtt_b", [128, 1024], BF16, kind="Internal").ap()
    wdt2_b = nc.dram_tensor("wdt2_b", [16, 128, 5632], BF16, kind="Internal").ap()
    wot_b = nc.dram_tensor("wot_b", [32, 128, 2048], BF16, kind="Internal").ap()
    cc_in = nc.dram_tensor("cc_in", [128, 4096], BF16, kind="Internal").ap()
    cc_out = nc.dram_tensor("cc_out", [1024, 4096], BF16, kind="Internal").ap()
    if debug:
        dbg_dt = nc.dram_tensor("dbg_dt", [128, 512], F32, kind="ExternalOutput").ap()
        dbg_ss = nc.dram_tensor("dbg_ss", [128, 4096], F32, kind="ExternalOutput").ap()
        dbg_yfm = nc.dram_tensor("dbg_yfm", [128, 16384], BF16, kind="ExternalOutput").ap()
        dbg_emat = nc.dram_tensor("dbg_emat", [128, 512], F32, kind="ExternalOutput").ap()
        dbg_decay = nc.dram_tensor("dbg_decay", [128, 512], BF16, kind="ExternalOutput").ap()
        dbg_expa = nc.dram_tensor("dbg_expa", [128, 512], F32, kind="ExternalOutput").ap()
        dbg_vst = nc.dram_tensor("dbg_vst", [128, 1024], BF16, kind="ExternalOutput").ap()
        dbg_xn = nc.dram_tensor("dbg_xn", [128, KD * XN_STRIDE], BF16, kind="ExternalOutput").ap()

    es = ExitStack()
    with es:
        def mk(stack):
            def sb(name, shape, dt=F32):
                return stack.enter_context(nc.sbuf_tensor(name, shape, dt))
            return sb

        sb = mk(es)

        def psf(name, shape, dt=F32):
            return es.enter_context(nc.psum_tensor(name, shape, dt))

        s_idb = sb("s_idb", [128, 128], BF16)
        s_ngt = sb("s_ngt", [128, 128], BF16)
        s_idx4 = sb("s_idx4", [128, 512], BF16)
        s_rep2 = sb("s_rep2", [128, 8192], BF16)
        s_ones2 = sb("s_ones2", [2, 128], BF16)
        s_onesc = sb("s_onesc", [128, 1])
        s_onesr = sb("s_onesr", [1, 128])
        s_convw = sb("s_convw", [128, 192])
        s_convb = sb("s_convb", [128, 48])
        s_nw = sb("s_nw", [128, 32])
        s_gs = sb("s_gs", [128, 44])
        s_us = sb("s_us", [128, 44])
        s_ds = sb("s_ds", [128, 16])
        s_alpha = sb("s_alpha", [128, 4])
        s_zero8 = sb("s_zero8", [128, 8])
        s_epsc = sb("s_epsc", [128, 1])
        s_hrows = sb("s_hrows", [1, 192])
        s_red = sb("s_red", [128, 16])
        s_alr = sb("s_alr", [1, 512])
        s_emat = sb("s_emat", [128, 512])
        s_expa = sb("s_expa", [128, 512])
        s_decay = sb("s_decay", [128, 512], BF16)
        s_vst = sb("s_vst", [128, 1024], BF16)
        s_dtb_b = sb("s_dtb_b", [128, 64])
        s_a_b = sb("s_a_b", [128, 64])
        s_d_b = sb("s_d_b", [128, 64])
        s_db16 = sb("s_db16", [128, 64], BF16)
        s_nwb = sb("s_nwb", [128, 32], BF16)
        s_ccb = sb("s_ccb", [128, 4096], BF16)

        pBIG = psf("pBIG", [128, 2048])
        pS1 = psf("pS1", [128, 512])
        pS2 = psf("pS2", [128, 512])
        pT = psf("pT", [128, 1024], BF16)

        L = SemT(nc, es, "L")
        Gd = SemT(nc, es, "Gd")
        P = SemT(nc, es, "P")
        V = SemT(nc, es, "V")
        Ac = SemT(nc, es, "Ac")
        Gp = SemT(nc, es, "Gp")
        St = SemT(nc, es, "St")
        Cc = SemT(nc, es, "Cc")
        Pk = SemT(nc, es, "Pk")
        V2 = SemT(nc, es, "V2")
        A2 = SemT(nc, es, "A2")
        Lx = SemT(nc, es, "Lx")
        Lr = SemT(nc, es, "Lr")
        Lr1 = SemT(nc, es, "Lr1")
        Lrld = SemT(nc, es, "Lrld")
        Lep = SemT(nc, es, "Lep")
        Ls0 = SemT(nc, es, "Ls0")
        Ls1 = SemT(nc, es, "Ls1")
        Gd0 = SemT(nc, es, "Gd0")
        Gd1 = SemT(nc, es, "Gd1")
        Gu0 = SemT(nc, es, "Gu0")
        Gu1 = SemT(nc, es, "Gu1")
        Dd0 = SemT(nc, es, "Dd0")
        Dd1 = SemT(nc, es, "Dd1")
        Sb = SemT(nc, es, "Sb")
        CVS = [SemT(nc, es, f"Cv{i}") for i in range(16)]

        block = es.enter_context(nc.Block())

        def VEC(f):
            @block.vector
            def _(v, __f=f):
                __f(Chain(v, V2))

        def SCA(f):
            @block.scalar
            def _(sc, __f=f):
                __f(Chain(sc, A2))

        def barrier():
            @block.tensor
            def _(t):
                t.wait_ge(P.h, P.n)
                P.inc(t.nop(nofuse=True))

            @block.vector
            def _(v):
                v.wait_ge(V2.h, V2.n)
                V.inc(v.nop(nofuse=True))

            @block.scalar
            def _(sc):
                sc.wait_ge(A2.h, A2.n)
                Ac.inc(sc.nop(nofuse=True))

            @block.gpsimd
            def _(g):
                g.wait_ge(Gp.h, Gp.n)
                Gp.inc(g.nop(nofuse=True))

            for name in ("tensor", "vector", "scalar", "gpsimd", "sync"):
                @getattr(block, name)
                def _(e):
                    P.wait(e); V.wait(e); Ac.wait(e); Gp.wait(e)
                    V2.wait(e); A2.wait(e)
                    L.wait(e); Gd.wait(e); St.wait(e); Pk.wait(e)
                    Lx.wait(e); Lr.wait(e); Lr1.wait(e); Lrld.wait(e); Lep.wait(e)
                    Ls0.wait(e); Ls1.wait(e); Gd0.wait(e); Gd1.wait(e)
                    Gu0.wait(e); Gu1.wait(e); Dd0.wait(e); Dd1.wait(e)
                    Sb.wait(e)

        @block.sync
        def _(s):
            for dst, src in [(s_idb, identb), (s_ngt, negtriT), (s_idx4, identx4),
                             (s_rep2, rep2), (s_ones2, ones2), (s_onesc, onescol),
                             (s_onesr, onesrow), (s_convw, convw), (s_convb, convb),
                             (s_nw, nwcol), (s_gs, gscol), (s_us, uscol),
                             (s_ds, dscol), (s_alpha, alphas), (s_hrows, hrows),
                             (s_zero8, zeros8), (s_epsc, epscol)]:
                L.dinc(s.dma_start(out=dst[:], in_=src))
        n_const = L.n
        cv_done = {}

        def emit_conversions(g, chunks, ci0):
            ci = ci0
            for name, src, dst, rngs in chunks:
                for a, b in rngs:
                    sem = CVS[ci]; ci += 1
                    if name == "wdtt":
                        sem.dinc(g.dma_start(out=dst, in_=src))
                    else:
                        sem.dinc(g.dma_start(out=dst[a:b], in_=src[a:b]))
                    for i in range(a, max(b, a + 1)):
                        cv_done[(name, i)] = (sem, sem.n)
            return ci

        @block.gpsimd
        def _(g):
            g.wait_ge(L.h, n_const)
            emit_conversions(g, [("wdtt", wdtt, wdtt_b, [(0, 0)]),
                                 ("wxt", wxt, wxt_b, [(0, 12)])], 0)
        for _ename in ("tensor", "vector", "scalar", "gpsimd"):
            @getattr(block, _ename)
            def _(e):
                e.wait_ge(L.h, n_const)

        with ExitStack() as scA:
            sbA = mk(scA)
            s_xn = sbA("s_xn", [128, KD * XN_STRIDE], BF16)
            s_ss = sbA("s_ss", [128, 4096])

            with ExitStack() as sc1:
                sb1 = mk(sc1)
                s_xtile = sb1("s_xtile", [128, 2 * D_MODEL])
                pT2 = sc1.enter_context(nc.psum_tensor("pT2a", [128, 1024], BF16))
                s_xnt = sb1("s_xnt", [128, 2 * D_MODEL], BF16)
                s_sqs = sb1("s_sqs", [128, D_MODEL])

                tile_rows = [tt * 128 for tt in range(8)] + [MH - 128]
                ln_ld = {}; ln_v = {}; ln_a = {}; ln_b = {}; ln_ap = {}; ln_tr = {}; ln_ev = {}
                LX = [Lx, Lr]
                for ti, r0 in enumerate(tile_rows):
                    pi = ti % 2

                    @block.sync
                    def _(s, ti=ti, r0=r0, pi=pi):
                        if ti >= 2 and (ti - 2) in ln_a:
                            s.wait_ge(Ac.h, ln_ap[ti - 2])
                        LX[pi].dinc(s.dma_start(out=s_xtile[:, pi * 2048:(pi + 1) * 2048],
                                                in_=xtm[r0:r0 + 128, :]))
                    ln_ld[ti] = (LX[pi], LX[pi].n)

                    @SCA
                    def _(sc, ti=ti, pi=pi):
                        sc.wait_ge(ln_ld[ti][0].h, ln_ld[ti][1])
                        xt = s_xtile[:, pi * 2048:(pi + 1) * 2048]
                        rb = pi * 8
                        sc.activation(s_sqs[:], xt, AF.Square, accum_out=s_red[:, rb + 1:rb + 2])
                        Ac.inc(sc.nop(nofuse=True))
                    ln_sq = Ac.n

                    @VEC
                    def _(v, ti=ti, pi=pi, ln_sq=ln_sq):
                        v.wait_ge(ln_ld[ti][0].h, ln_ld[ti][1])
                        v.wait_ge(Ac.h, ln_sq)
                        xt = s_xtile[:, pi * 2048:(pi + 1) * 2048]
                        rb = pi * 8
                        v.reduce_sum(s_red[:, rb:rb + 1], xt, axis=mybir.AxisListType.X)
                        v.tensor_scalar_mul(s_red[:, rb + 2:rb + 3], s_red[:, rb:rb + 1], 1.0 / D_MODEL)
                        v.tensor_mul(s_red[:, rb + 3:rb + 4], s_red[:, rb + 2:rb + 3], s_red[:, rb + 2:rb + 3])
                        v.scalar_tensor_tensor(s_red[:, rb + 4:rb + 5], s_red[:, rb + 1:rb + 2], 1.0 / D_MODEL,
                                               s_red[:, rb + 3:rb + 4], ALU.mult, ALU.subtract)
                        V.inc(v.nop(nofuse=True))
                    ln_v[ti] = V.n

                    @SCA
                    def _(sc, ti=ti, pi=pi):
                        sc.wait_ge(V.h, ln_v[ti])
                        rb = pi * 8
                        sc.activation(s_red[:, rb + 5:rb + 6], s_red[:, rb + 4:rb + 5], AF.Ln,
                                      bias=s_epsc[:, 0:1])
                        sc.activation(s_red[:, rb + 5:rb + 6], s_red[:, rb + 5:rb + 6], AF.Exp, scale=-0.5)
                        Ac.inc(sc.nop(nofuse=True))
                    ln_a[ti] = Ac.n

                    @VEC
                    def _(v, ti=ti, pi=pi):
                        v.wait_ge(Ac.h, ln_a[ti])
                        rb = pi * 8
                        v.scalar_tensor_tensor(s_red[:, rb + 6:rb + 7], s_red[:, rb + 2:rb + 3], -1.0,
                                               s_red[:, rb + 5:rb + 6], ALU.mult, ALU.mult)
                        V.inc(v.nop(nofuse=True))
                    ln_b[ti] = V.n

                    @SCA
                    def _(sc, ti=ti, pi=pi):
                        sc.wait_ge(V.h, ln_b[ti])
                        if ti >= 2 and (ti - 2) in ln_tr:
                            sc.wait_ge(P.h, ln_tr[ti - 2])
                        rb = pi * 8
                        sc.activation(s_xnt[:, pi * 2048:(pi + 1) * 2048],
                                      s_xtile[:, pi * 2048:(pi + 1) * 2048], AF.Identity,
                                      scale=s_red[:, rb + 5:rb + 6], bias=s_red[:, rb + 6:rb + 7])
                        Ac.inc(sc.nop(nofuse=True))
                    ln_ap[ti] = Ac.n

                    for grp in range(4):
                        pTx = pT if grp % 2 == 0 else pT2

                        @block.tensor
                        def _(t, ti=ti, grp=grp, pi=pi, pTx=pTx):
                            t.wait_ge(Ac.h, ln_ap[ti])
                            key = (ti, grp - 2) if grp >= 2 else (ti - 1, grp + 2)
                            if key in ln_ev:
                                t.wait_ge(V.h, ln_ev[key])
                            for j in range(4):
                                dtile = grp * 4 + j
                                mm = t.transpose(pTx[:, j * 128:(j + 1) * 128],
                                                 s_xnt[:, pi * 2048 + dtile * 128: pi * 2048 + (dtile + 1) * 128],
                                                 s_idb[:])
                            P.inc(mm)
                        ln_tr[(ti, grp)] = P.n

                        @VEC
                        def _(v, ti=ti, grp=grp, r0=r0, pTx=pTx):
                            v.wait_ge(P.h, ln_tr[(ti, grp)])
                            dst = ap3(s_xn, (grp * 4) * XN_STRIDE + r0, (XN_STRIDE, 4), (1, 128))
                            v.tensor_copy(dst, pTx.ap()[:, 0:512].rearrange(
                                "p (j c) -> p j c", j=4))
                            V.inc(v.nop(nofuse=True))
                        ln_ev[(ti, grp)] = V.n
                    ln_tr[ti] = P.n

                @VEC
                def _(v):
                    V.wait(v)
                    pad = ap3(s_xn, 1027, (XN_STRIDE, KD), (1, XN_STRIDE - 1027))
                    src = ap3(s_zero8, 0, (0, KD), (1, XN_STRIDE - 1027))
                    v.tensor_copy(pad, src)
                    V.inc(v.nop(nofuse=True))

            barrier()

            @block.gpsimd
            def _(g):
                emit_conversions(g, [("wxt", wxt, wxt_b,
                                      [(12, 24), (24, 36), (36, 48)])], 2)

            with ExitStack() as sc2:
                sb2 = mk(sc2)
                s_tri = sb2("s_tri", [128, 128])
                s_triu = sb2("s_triu", [128, 128])
                s_idf = sb2("s_idf", [128, 128])
                s_dt = sb2("s_dt", [128, 512])
                s_lndt = sb2("s_lndt", [128, 512])
                s_adt = sb2("s_adt", [128, 512])
                s_vtm = sb2("s_vtm", [128, 512])
                s_acsfm = sb2("s_acsfm", [64, 1024])
                s_acshi = sb2("s_acshi", [64, 1024], BF16)
                s_acslo = sb2("s_acslo", [64, 1024], BF16)
                s_vfm = sb2("s_vfm", [64, 128])
                s_wdt = sb2("s_wdt", [128, KD * 64], BF16)
                pROW = sc2.enter_context(nc.psum_tensor("pROW", [1, 512], F32))

                @block.sync
                def _(s):
                    for dst, src in [(s_tri, trif), (s_triu, triuf), (s_idf, identf)]:
                        L.dinc(s.dma_start(out=dst[:], in_=src))

                @block.scalar
                def _(sc):
                    sc.wait_ge(cv_done[("wdtt", 0)][0].h, cv_done[("wdtt", 0)][1])
                    Gd.dinc(sc.dma_start(out=s_wdt[:], in_=wdtt_b))

                @block.tensor
                def _(t):
                    L.wait(t)
                    t.matmul(pS1[:, 0:64], s_onesr[:], s_hrows[:, 0:64], start=True, stop=True)
                    t.matmul(pS1[:, 64:128], s_onesr[:], s_hrows[:, 64:128], start=True, stop=True)
                    mm = t.matmul(pS1[:, 128:192], s_onesr[:], s_hrows[:, 128:192], start=True, stop=True)
                    P.inc(mm)

                @VEC
                def _(v):
                    P.wait(v)
                    v.tensor_copy(s_dtb_b[:], pS1[:, 0:64])
                    v.tensor_copy(s_a_b[:], pS1[:, 64:128])
                    v.tensor_copy(s_d_b[:], pS1[:, 128:192])
                    v.tensor_copy(s_db16[:], pS1[:, 128:192])
                    v.tensor_copy(s_nwb[:], s_nw[:])
                    V.inc(v.nop(nofuse=True))

                @block.tensor
                def _(t):
                    Gd.wait(t)
                    V.wait(t)
                    for tt in range(8):
                        for k in range(KD):
                            lhsT = s_xn[:, k * XN_STRIDE + 3 + tt * 128: k * XN_STRIDE + 3 + tt * 128 + 128]
                            mm = t.matmul(pBIG[:, tt * 64:(tt + 1) * 64], lhsT,
                                          s_wdt[:, k * 64:(k + 1) * 64],
                                          start=(k == 0), stop=(k == KD - 1))
                    P.inc(mm)

                @VEC
                def _(v):
                    P.wait(v)
                    db_ap = ap3(s_dtb_b, 0, (0, 8), (1, 64))
                    v.tensor_tensor(s_dt.ap().rearrange("p (c h) -> p c h", c=8),
                                    pBIG.ap()[:, 0:512].rearrange("p (c h) -> p c h", c=8),
                                    db_ap, ALU.add)
                    v.tensor_scalar_mul(s_vtm[:], s_dt[:], -1.0)
                    v.tensor_max(s_vtm[:], s_vtm[:], s_dt[:])
                    V.inc(v.nop(nofuse=True))

                @SCA
                def _(sc):
                    V.wait(sc)
                    sc.activation(s_lndt[:], s_vtm[:], AF.Exp, scale=-1.0)
                    sc.activation(s_lndt[:], s_lndt[:], AF.Ln, bias=1.0)
                    Ac.inc(sc.nop(nofuse=True))

                @VEC
                def _(v):
                    Ac.wait(v)
                    v.tensor_scalar_max(s_adt[:], s_dt[:], 0.0)
                    v.tensor_add(s_dt[:], s_adt[:], s_lndt[:])
                    V.inc(v.nop(nofuse=True))

                @SCA
                def _(sc):
                    V.wait(sc)
                    sc.activation(s_lndt[:], s_dt[:], AF.Ln)
                    Ac.inc(sc.nop(nofuse=True))

                @VEC
                def _(v):
                    Ac.wait(v)
                    a_ap = ap3(s_a_b, 0, (0, 8), (1, 64))
                    v.tensor_tensor(s_adt.ap().rearrange("p (c h) -> p c h", c=8),
                                    s_dt.ap().rearrange("p (c h) -> p c h", c=8),
                                    a_ap, ALU.mult)
                    V.inc(v.nop(nofuse=True))

                @block.tensor
                def _(t):
                    V.wait(t)
                    for c in range(NCH):
                        adt_c = s_adt[:, c * 64:(c + 1) * 64]
                        lndt_c = s_lndt[:, c * 64:(c + 1) * 64]
                        t.matmul(pBIG[:, 512 + c * 64: 512 + (c + 1) * 64], s_tri[:], adt_c,
                                 start=True, stop=True)
                        t.matmul(pROW[:, c * 64:(c + 1) * 64], s_onesc[:], adt_c,
                                 start=True, stop=True)
                        t.matmul(pBIG[0:64, 1024 + c * 128: 1024 + (c + 1) * 128], adt_c, s_tri[:],
                                 start=True, stop=True)
                        t.matmul(pS2[:, c * 64:(c + 1) * 64], s_triu[:], adt_c,
                                 start=True, stop=False)
                        mm = t.matmul(pS2[:, c * 64:(c + 1) * 64], s_idf[:], lndt_c,
                                      start=False, stop=True)
                    P.inc(mm)

                @VEC
                def _(v):
                    P.wait(v)
                    v.tensor_sub(s_vtm[:], s_lndt[:], pBIG[:, 512:1024])
                    v.tensor_copy(s_acsfm[:], pBIG[0:64, 1024:2048])
                    v.tensor_copy(s_alr[:], pROW[:])
                    V.inc(v.nop(nofuse=True))

                @SCA
                def _(sc):
                    V.wait(sc)
                    sc.activation(s_expa[:], pBIG[:, 512:1024], AF.Exp)
                    sc.activation(s_decay[:], pS2[:], AF.Exp)
                    sc.activation(s_alr[:], s_alr[:], AF.Exp)
                    Ac.inc(sc.nop(nofuse=True))

                @block.tensor
                def _(t):
                    Ac.wait(t)
                    mm = t.matmul(pBIG[:, 0:512], s_onesr[:], s_alr[:], start=True, stop=True)
                    P.inc(mm)
                emat_p = P.n

                @block.tensor
                def _(t):
                    V.wait(t)
                    for c in range(NCH):
                        mm = t.transpose(pBIG[0:64, 1024 + c * 128: 1024 + (c + 1) * 128],
                                         s_vtm[:, c * 64:(c + 1) * 64], s_idf[:])
                    P.inc(mm)

                @VEC
                def _(v):
                    P.wait(v)
                    v.tensor_copy(s_emat[:], pBIG[:, 0:512])
                    v.tensor_copy(s_acshi[:], s_acsfm[:])
                    v.scalar_tensor_tensor(s_acslo[:], s_acshi[:], -1.0, s_acsfm[:],
                                           ALU.mult, ALU.add)
                    v.tensor_copy(s_vst[0:64, :], pBIG[0:64, 1024:2048])
                    v.scalar_tensor_tensor(s_vst[64:128, :], s_vst[0:64, :], -1.0,
                                           pBIG[0:64, 1024:2048], ALU.mult, ALU.add)
                    V.inc(v.nop(nofuse=True))

                @block.sync
                def _(s):
                    V.wait(s)
                    Pk.dinc(s.dma_start(out=acs_hi_d, in_=s_acshi[:]))
                    Pk.dinc(s.dma_start(out=acs_lo_d, in_=s_acslo[:]))
                    if debug:
                        St.dinc(s.dma_start(out=dbg_dt, in_=s_dt[:]))
                        St.dinc(s.dma_start(out=dbg_emat, in_=s_emat[:]))
                        St.dinc(s.dma_start(out=dbg_decay, in_=s_decay[:]))
                        St.dinc(s.dma_start(out=dbg_expa, in_=s_expa[:]))
                        St.dinc(s.dma_start(out=dbg_vst, in_=s_vst[:]))

            barrier()

            with ExitStack() as sc3:
                sb3 = mk(sc3)
                s_xtm = sb3("s_xtm", [128, 16384], BF16)
                s_bfm = sb3("s_bfm", [128, 4096], BF16)
                s_cfm = sb3("s_cfm", [128, 4096], BF16)
                s_btm = sb3("s_btm", [128, 4096], BF16)
                s_xr = sb3("s_xr", [128, 2 * 1032])
                s_cv = sb3("s_cv", [128, 2 * 1024])
                s_cu = sb3("s_cu", [128, 1024])
                s_xf = sb3("s_xf", [128, 2 * 1024], BF16)
                s_xdec = sb3("s_xdec", [128, 2048], BF16)
                s_stc = sb3("s_stc", [128, 2048])
                s_cu2 = sb3("s_cu2", [128, 1024])
                s_wx = sb3("s_wx", [128, 2 * 2048], BF16)
                pT2 = sc3.enter_context(nc.psum_tensor("pT2b", [128, 1024], BF16))

                xslot_free = [0, 0]
                xgd_done = {}
                for hp in range(2):
                    if hp == 1:
                        vh0 = V.n

                        @block.gpsimd
                        def _(g, vh0=vh0):
                            g.wait_ge(V.h, vh0)
                            emit_conversions(g, [("wzt", wzt, wzt_b, [(0, 4), (4, 8)])], 5)
                    tiles = []
                    for xi in range(16):
                        tiles.append(((2048 * hp + xi * 128) // 128, "x", xi))
                    for bi in range(4):
                        tiles.append(((4096 + 512 * hp + bi * 128) // 128, "b", bi))
                    for ci in range(4):
                        tiles.append(((5120 + 512 * hp + ci * 128) // 128, "c", ci))

                    xb_mm = {}; xb_ev = {}; xb_cv = {}; xb_si = {}; xb_tr = {}; xb_xe = {}
                    for tix, (ct, kind, idx) in enumerate(tiles):
                        slot = tix % 2
                        if slot == 0:
                            pdst = [pBIG[:, 0:343], pBIG[:, 512:855], pBIG[:, 1024:1367]]
                        else:
                            pdst = [pBIG[:, 1536:1879], pS1[:, 0:343], pS2[:, 0:343]]
                        xro = slot * 1032
                        cvo = slot * 1024
                        xfo = slot * 1024

                        @block.scalar
                        def _(sc, ct=ct, slot=slot):
                            sc.wait_ge(cv_done[("wxt", ct)][0].h, cv_done[("wxt", ct)][1])
                            sc.wait_ge(P.h, xslot_free[slot])
                            (Gd0 if slot == 0 else Gd1).dinc(sc.dma_start(
                                out=s_wx[:, slot * 2048:(slot + 1) * 2048],
                                in_=wxt_b[ct]))
                        xgd_done[(hp, tix)] = (Gd0 if slot == 0 else Gd1, (Gd0 if slot == 0 else Gd1).n)

                        @block.tensor
                        def _(t, slot=slot, hp=hp, tix=tix, pdst=pdst):
                            t.wait_ge(xgd_done[(hp, tix)][0].h, xgd_done[(hp, tix)][1])
                            if tix >= 2:
                                t.wait_ge(V.h, xb_ev[tix - 2])
                            elif hp == 1:
                                V.wait(t)
                            for tc in range(3):
                                for k in range(KD):
                                    lhsT = s_wx[:, slot * 2048 + k * 128: slot * 2048 + (k + 1) * 128]
                                    rhs = s_xn[:, k * XN_STRIDE + tc * 343: k * XN_STRIDE + (tc + 1) * 343]
                                    mm = t.matmul(pdst[tc], lhsT, rhs,
                                                  start=(k == 0), stop=(k == KD - 1))
                            P.inc(mm)
                        xslot_free[slot] = P.n
                        xb_mm[tix] = P.n

                        @VEC
                        def _(v, tix=tix, slot=slot, xro=xro, ct=ct, cvo=cvo):
                            v.wait_ge(P.h, xb_mm[tix])
                            if tix >= 2:
                                v.wait_ge(Ac.h, xb_si[tix - 2])
                            if slot == 0:
                                src = bass.AP(tensor=pBIG, offset=0,
                                              ap=[[pBIG.ap().ap[0][0], 128], [512, 3], [1, 343]])
                                v.tensor_copy(
                                    s_xr.ap()[:, xro:xro + 1029].rearrange("p (t c) -> p t c", t=3), src)
                            else:
                                v.tensor_copy(s_xr[:, xro:xro + 343], pBIG[:, 1536:1879])
                                v.tensor_copy(s_xr[:, xro + 343:xro + 686], pS1[:, 0:343])
                                v.tensor_copy(s_xr[:, xro + 686:xro + 1029], pS2[:, 0:343])
                            v.tensor_scalar_mul(s_cv[:, cvo:cvo + 1024], s_xr[:, xro:xro + 1024],
                                                s_convw[:, ct * 4: ct * 4 + 1])
                            for kk in range(1, 4):
                                v.scalar_tensor_tensor(s_cv[:, cvo:cvo + 1024],
                                                       s_xr[:, xro + kk:xro + kk + 1024],
                                                       s_convw[:, ct * 4 + kk: ct * 4 + kk + 1],
                                                       s_cv[:, cvo:cvo + 1024], ALU.mult, ALU.add)
                            V.inc(v.nop(nofuse=True))
                        xb_ev[tix] = V.n
                        xb_cv[tix] = V.n

                        @SCA
                        def _(sc, tix=tix, ct=ct, kind=kind, idx=idx, cvo=cvo, xfo=xfo):
                            sc.wait_ge(V.h, xb_cv[tix])
                            if kind == "x":
                                if tix >= 2 and (tix - 2) in xb_tr:
                                    sc.wait_ge(P.h, xb_tr[tix - 2])
                                dst = s_xf[:, xfo:xfo + 1024]
                            elif kind == "b":
                                dst = s_bfm[:, idx * 1024:(idx + 1) * 1024]
                            else:
                                dst = s_cfm[:, idx * 1024:(idx + 1) * 1024]
                            sc.activation(dst, s_cv[:, cvo:cvo + 1024], AF.Silu,
                                          bias=s_convb[:, ct:ct + 1])
                            Ac.inc(sc.nop(nofuse=True))
                        xb_si[tix] = Ac.n

                        if kind == "x":
                            for half in range(2):
                                pTx = pT if half % 2 == 0 else pT2

                                @block.tensor
                                def _(t, tix=tix, half=half, xfo=xfo, pTx=pTx):
                                    t.wait_ge(Ac.h, xb_si[tix])
                                    key = (tix - 2, half)
                                    if key in xb_xe:
                                        t.wait_ge(V.h, xb_xe[key])
                                    for j in range(4):
                                        c = half * 4 + j
                                        mm = t.transpose(pTx[:, j * 128:(j + 1) * 128],
                                                         s_xf[:, xfo + c * 128: xfo + (c + 1) * 128],
                                                         s_idb[:])
                                    P.inc(mm)
                                xb_tr[(tix, half)] = P.n

                                @VEC
                                def _(v, tix=tix, half=half, idx=idx, pTx=pTx):
                                    v.wait_ge(P.h, xb_tr[(tix, half)])
                                    dst = ap3(s_xtm, (half * 4) * 2048 + idx * 128, (2048, 4), (1, 128))
                                    v.tensor_copy(dst, pTx.ap()[:, 0:512].rearrange(
                                        "p (j c) -> p j c", j=4))
                                    V.inc(v.nop(nofuse=True))
                                xb_xe[(tix, half)] = V.n
                            xb_tr[tix] = P.n

                    for bi in range(4):
                        for half in range(2):
                            @block.tensor
                            def _(t, bi=bi, half=half):
                                Ac.wait(t)
                                V.wait(t)
                                pTx = pT if half % 2 == 0 else pT2
                                for j in range(4):
                                    c = half * 4 + j
                                    mm = t.transpose(pTx[:, j * 128:(j + 1) * 128],
                                                     s_bfm[:, bi * 1024 + c * 128: bi * 1024 + (c + 1) * 128],
                                                     s_idb[:])
                                P.inc(mm)

                            @VEC
                            def _(v, bi=bi, half=half):
                                P.wait(v)
                                pTx = pT if half % 2 == 0 else pT2
                                dst = ap3(s_btm, (half * 4) * 512 + bi * 128, (512, 4), (1, 128))
                                v.tensor_copy(dst, pTx.ap()[:, 0:512].rearrange("p (j c) -> p j c", j=4))
                                V.inc(v.nop(nofuse=True))

                    for c in range(NCH):
                        @VEC
                        def _(v, c=c, hp=hp):
                            V.wait(v)
                            Pk.wait(v)
                            dec = ap3(s_decay, c * 64 + 32 * hp, (1, 32), (0, 64))
                            v.tensor_tensor(
                                s_xdec.ap().rearrange("p (h q) -> p h q", h=32),
                                ap3(s_xtm, c * 2048, (64, 32), (1, 64)),
                                dec, ALU.mult)
                            V.inc(v.nop(nofuse=True))
                        st_xd = V.n

                        @block.tensor
                        def _(t, c=c, st_xd=st_xd):
                            t.wait_ge(V.h, st_xd)
                            pds = [pS1.ap(), pS2.ap(), pBIG[:, 0:512], pBIG[:, 512:1024]]
                            for gl in range(4):
                                ps = pds[gl]
                                for hh in range(8):
                                    mm = t.matmul(ps[:, hh * 64:(hh + 1) * 64],
                                                  s_btm[:, c * 512 + gl * 128: c * 512 + (gl + 1) * 128],
                                                  s_xdec[:, (gl * 8 + hh) * 64:(gl * 8 + hh + 1) * 64],
                                                  start=True, stop=True)
                            P.inc(mm)
                        st_mm = P.n

                        @VEC
                        def _(v, c=c, hp=hp, st_mm=st_mm):
                            v.wait_ge(P.h, st_mm)
                            v.tensor_copy(s_stc[:, 0:512], pS1[:, 0:512])
                            v.tensor_copy(s_stc[:, 512:1024], pS2[:, 0:512])
                            v.tensor_copy(s_stc[:, 1024:2048], pBIG[:, 0:1024])
                            sl = slice(2048 * hp, 2048 * hp + 2048)
                            if c == 0:
                                v.tensor_copy(s_ss[:, sl], s_stc[:])
                            else:
                                em = ap3(s_emat, c * 64 + 32 * hp, (1, 32), (0, 64))
                                v.tensor_tensor(
                                    s_ss.ap()[:, sl].rearrange("p (h q) -> p h q", h=32),
                                    s_ss.ap()[:, sl].rearrange("p (h q) -> p h q", h=32),
                                    em, ALU.mult)
                                v.tensor_add(s_ss[:, sl], s_ss[:, sl], s_stc[:])
                            V.inc(v.nop(nofuse=True))
                        st_ev = V.n

                        @block.sync
                        def _(s, c=c, hp=hp, st_ev=st_ev):
                            s.wait_ge(V.h, st_ev)
                            Pk.dinc(s.dma_start(out=states_d[hp * NCH + c], in_=s_stc[:]))

                    @block.sync
                    def _(s, hp=hp):
                        V.wait(s)
                        Ac.wait(s)
                        P.wait(s, P.n)
                        Pk.dinc(s.dma_start(out=xtm_park[hp], in_=s_xtm[:]))
                        Pk.dinc(s.dma_start(out=b_park[hp], in_=s_bfm[:]))
                        Pk.dinc(s.dma_start(out=c_park[hp], in_=s_cfm[:]))

                    @VEC
                    def _(v):
                        Pk.wait(v)
                        V.inc(v.nop(nofuse=True))

            vh36 = V.n

            @block.gpsimd
            def _(g, vh36=vh36):
                g.wait_ge(V.h, vh36)
                emit_conversions(g, [("wdt2", wdt2, wdt2_b, [(0, 8), (8, 16)])], 7)

            @VEC
            def _(v):
                v.tensor_copy(s_ccb[:], s_ss[:])
                V.inc(v.nop(nofuse=True))

            @block.sync
            def _(s):
                V.wait(s)
                L.dinc(s.dma_start(out=cc_in, in_=s_ccb[:]))
                if debug:
                    St.dinc(s.dma_start(out=dbg_ss, in_=s_ss[:]))

            @block.gpsimd
            def _(g):
                L.wait(g)
                Cc.inc(g.collective_compute(
                    "AllGather", ALU.bypass,
                    replica_groups=[list(range(8))],
                    ins=[cc_in.opt()], outs=[cc_out.opt()],
                ))

            barrier()

            with ExitStack() as scz:
                sbz = mk(scz)
                s_wz = sbz("s_wz", [128, 2 * 8192], BF16)
                s_z = sbz("s_z", [128, 16384], BF16)

                def emit_s9s95():
                    with ExitStack() as sc9:
                        sb9 = mk(sc9)
                        s_sin = sb9("s_sin", [128, 4096])

                        for r in range(4):
                            @block.sync
                            def _(s, r=r):
                                if r == 0:
                                    Cc.wait(s, 1)
                                else:
                                    V.wait(s)
                                L.dinc(s.dma_start(out=s_ccb[:],
                                                   in_=cc_out[r * 256:r * 256 + 128, :]))

                            @VEC
                            def _(v, r=r):
                                L.wait(v)
                                if r == 0:
                                    v.tensor_scalar_mul(s_sin[:], s_ccb[:], s_alpha[:, 0:1])
                                else:
                                    v.scalar_tensor_tensor(s_sin[:], s_ccb[:], s_alpha[:, r:r + 1],
                                                           s_sin[:], ALU.mult, ALU.add)
                                V.inc(v.nop(nofuse=True))

                        @block.sync
                        def _(s):
                            V.wait(s)
                            Sb.dinc(s.dma_start(out=sin_d, in_=s_sin[:]))

                    with ExitStack() as sc95:
                        sb95 = mk(sc95)
                        s_pss = sb95("s_pss", [128, 2048])
                        s_pst = sb95("s_pst", [128, 2 * 2048])
                        s_pssb = sb95("s_pssb", [128, 2048], BF16)

                        for hp in range(2):
                            @block.sync
                            def _(s, hp=hp):
                                if hp == 0:
                                    Sb.wait(s)
                                else:
                                    V.wait(s)
                                Lrld.dinc(s.dma_start(out=s_pss[:],
                                                      in_=sin_d[:, 2048 * hp: 2048 * hp + 2048]))
                                Lrld.dinc(s.dma_start(out=ap3(s_pst, 0, (2048, 1), (1, 2048)),
                                                      in_=states_d[hp * NCH + 0]))

                            for c in range(NCH):
                                @block.sync
                                def _(s, c=c, hp=hp):
                                    if c >= 1:
                                        V.wait(s)
                                    if c + 1 < NCH:
                                        Lrld.dinc(s.dma_start(
                                            out=ap3(s_pst, ((c + 1) % 2) * 2048, (2048, 1), (1, 2048)),
                                            in_=states_d[hp * NCH + c + 1]))

                                @VEC
                                def _(v, c=c, hp=hp):
                                    Lrld.wait(v)
                                    Sb.wait(v)
                                    v.tensor_copy(s_pssb[:], s_pss[:])
                                    em = ap3(s_emat, c * 64 + 32 * hp, (1, 32), (0, 64))
                                    v.tensor_tensor(s_pss.ap().rearrange("p (h q) -> p h q", h=32),
                                                    s_pss.ap().rearrange("p (h q) -> p h q", h=32),
                                                    em, ALU.mult)
                                    v.tensor_add(s_pss[:], s_pss[:],
                                                 s_pst[:, (c % 2) * 2048:((c % 2) + 1) * 2048])
                                    V.inc(v.nop(nofuse=True))

                                @block.sync
                                def _(s, c=c, hp=hp):
                                    V.wait(s)
                                    Sb.dinc(s.dma_start(out=ssb_d[hp * NCH + c], in_=s_pssb[:]))

                zslot_free = [0, 0]
                zgd_done = {}
                z_ac = {}
                for hp in range(2):
                    for zc in range(4):
                        slot = (hp * 4 + zc) % 2
                        six = hp * 4 + zc

                        @block.scalar
                        def _(sc, six=six, slot=slot):
                            sc.wait_ge(cv_done[("wzt", six)][0].h, cv_done[("wzt", six)][1])
                            sc.wait_ge(P.h, zslot_free[slot])
                            (Gd0 if slot == 0 else Gd1).dinc(sc.dma_start(
                                out=s_wz[:, slot * 8192:(slot + 1) * 8192],
                                in_=wzt_b[six]))
                        zgd_done[six] = (Gd0 if slot == 0 else Gd1, (Gd0 if slot == 0 else Gd1).n)

                        for tt in range(8):
                            zseq = six * 8 + tt

                            @block.tensor
                            def _(t, tt=tt, slot=slot, six=six, zseq=zseq):
                                t.wait_ge(zgd_done[six][0].h, zgd_done[six][1])
                                if zseq >= 2:
                                    t.wait_ge(Ac.h, z_ac[zseq - 2])
                                ps = pS1 if tt % 2 == 0 else pS2
                                for k in range(KD):
                                    lhsT = s_xn[:, k * XN_STRIDE + 3 + tt * 128: k * XN_STRIDE + 3 + tt * 128 + 128]
                                    mm = t.matmul(ps[:, 0:512], lhsT,
                                                  s_wz[:, slot * 8192 + k * 512: slot * 8192 + (k + 1) * 512],
                                                  start=(k == 0), stop=(k == KD - 1))
                                P.inc(mm)

                            @SCA
                            def _(sc, zc=zc, tt=tt):
                                P.wait(sc)
                                ps = pS1 if tt % 2 == 0 else pS2
                                sc.activation(
                                    s_z[:, tt * 2048 + zc * 512: tt * 2048 + (zc + 1) * 512],
                                    ps[:, 0:512], AF.Silu)
                                Ac.inc(sc.nop(nofuse=True))
                            z_ac[zseq] = Ac.n

                        zslot_free[slot] = P.n

                    @block.sync
                    def _(s, hp=hp):
                        Ac.wait(s)
                        Pk.dinc(s.dma_start(out=z_park[hp], in_=s_z[:]))

                    @SCA
                    def _(sc):
                        Pk.wait(sc)
                        Ac.inc(sc.nop(nofuse=True))

                    if hp == 0:
                        ah_z0 = Ac.n

                        @block.gpsimd
                        def _(g, ah_z0=ah_z0):
                            g.wait_ge(Ac.h, ah_z0)
                            emit_conversions(g, [("wot", wot, wot_b, [(0, 16), (16, 32)])], 9)

                        emit_s9s95()

            barrier()

            with ExitStack() as sc8:
                sb8 = mk(sc8)
                s_wg = sb8("s_wg", [128, 3 * 2048], BF16)
                s_wu = sb8("s_wu", [128, 2 * 2048], BF16)
                s_wd = sb8("s_wd", [128, 2 * 5632], BF16)
                s_wgf = sb8("s_wgf", [128, 2 * 2048])
                s_hdn = sb8("s_hdn", [128, 44 * 512], BF16)
                s_sg = sb8("s_sg", [128, 1024])
                s_epg = sb8("s_epg", [128, 512], BF16)

                g3_free = [0, 0, 0]
                uslot_free = [0, 0]
                gcast_done = {}
                dslot_free = [0, 0]
                dgd_done = {}
                ffn_pg = {}; ffn_ac = {}; ffn_v = {}
                down_ac = {}
                pend_stt = None
                for tc in range(2):
                    for it in range(44):
                        slot = it % 2
                        j = tc * 44 + it
                        gs = j % 3

                        if slot == 0:
                            @block.scalar
                            def _(sc, it=it, slot=slot, j=j):
                                if j >= 2:
                                    sc.wait_ge(V.h, gcast_done[j - 2])
                                Gd0.dinc(sc.dma_start(
                                    out=s_wgf[:, slot * 2048:(slot + 1) * 2048], in_=wgt[it]))
                        else:
                            @block.sync
                            def _(s, it=it, slot=slot, j=j):
                                if j >= 2:
                                    s.wait_ge(V.h, gcast_done[j - 2])
                                Gd1.dinc(s.dma_start(
                                    out=s_wgf[:, slot * 2048:(slot + 1) * 2048], in_=wgt[it]))
                        gld_g = ((Gd0 if slot == 0 else Gd1), (Gd0 if slot == 0 else Gd1).n)

                        @block.gpsimd
                        def _(g, it=it, slot=slot, j=j):
                            g.wait_ge(P.h, uslot_free[slot])
                            semu = Gu0 if slot == 0 else Gu1
                            semu.dinc(g.dma_start(
                                out=s_wu[:, slot * 2048:(slot + 1) * 2048], in_=wut[it]))
                        gld_u = ((Gu0 if slot == 0 else Gu1), (Gu0 if slot == 0 else Gu1).n)

                        @VEC
                        def _(v, gs=gs, slot=slot, j=j, gld_g=gld_g):
                            v.wait_ge(gld_g[0].h, gld_g[1])
                            v.wait_ge(P.h, g3_free[gs])
                            v.tensor_copy(s_wg[:, gs * 2048:(gs + 1) * 2048],
                                          s_wgf[:, slot * 2048:(slot + 1) * 2048])
                            V.inc(v.nop(nofuse=True))
                        gcast_done[j] = V.n

                        if pend_stt is not None:
                            pend_stt()
                        pend_stt = None

                        pg = pBIG[:, 0:512] if j % 2 == 0 else pS1[:, 0:512]
                        pu = pBIG[:, 512:1024] if j % 2 == 0 else pS2[:, 0:512]

                        @block.tensor
                        def _(t, tc=tc, slot=slot, gs=gs, it=it, j=j, pg=pg, pu=pu,
                              gld_u=gld_u):
                            t.wait_ge(V.h, gcast_done[j])
                            t.wait_ge(gld_u[0].h, gld_u[1])
                            if j >= 2:
                                t.wait_ge(Ac.h, ffn_ac[j - 2])
                                t.wait_ge(V.h, ffn_v[j - 2])
                            for k in range(KD):
                                rhs = s_xn[:, k * XN_STRIDE + 3 + tc * 512: k * XN_STRIDE + 3 + (tc + 1) * 512]
                                mm = t.matmul(pg,
                                              s_wg[:, gs * 2048 + k * 128: gs * 2048 + (k + 1) * 128],
                                              rhs, start=(k == 0), stop=(k == KD - 1))
                            P.inc(mm)
                            ffn_pg[j] = P.n
                            for k in range(KD):
                                rhs = s_xn[:, k * XN_STRIDE + 3 + tc * 512: k * XN_STRIDE + 3 + (tc + 1) * 512]
                                mm = t.matmul(pu,
                                              s_wu[:, slot * 2048 + k * 128: slot * 2048 + (k + 1) * 128],
                                              rhs, start=(k == 0), stop=(k == KD - 1))
                            P.inc(mm)
                        g3_free[gs] = P.n
                        uslot_free[slot] = P.n
                        ffn_pu = P.n

                        @SCA
                        def _(sc, it=it, j=j, pg=pg):
                            sc.wait_ge(P.h, ffn_pg[j])
                            sc.activation(s_sg[:, (j % 2) * 512:(j % 2) * 512 + 512],
                                          pg, AF.Silu, scale=s_gs[:, it:it + 1])
                            Ac.inc(sc.nop(nofuse=True))
                        ffn_ac[j] = Ac.n

                        def _mk_stt(it=it, j=j, pu=pu, _pun=ffn_pu):
                            @VEC
                            def _(v):
                                v.wait_ge(P.h, _pun)
                                v.wait_ge(Ac.h, ffn_ac[j])
                                v.scalar_tensor_tensor(s_hdn[:, it * 512:(it + 1) * 512],
                                                       pu, s_us[:, it:it + 1],
                                                       s_sg[:, (j % 2) * 512:(j % 2) * 512 + 512],
                                                       ALU.mult, ALU.mult)
                                V.inc(v.nop(nofuse=True))
                            ffn_v[j] = V.n
                        pend_stt = _mk_stt

                    if pend_stt is not None:
                        pend_stt()
                    pend_stt = None

                    for dt_ in range(16):
                        slot = dt_ % 2

                        @block.scalar
                        def _(sc, dt_=dt_, slot=slot):
                            sc.wait_ge(cv_done[("wdt2", dt_)][0].h, cv_done[("wdt2", dt_)][1])
                            sc.wait_ge(P.h, dslot_free[slot])
                            (Dd0 if slot == 0 else Dd1).dinc(sc.dma_start(
                                out=s_wd[:, slot * 5632:(slot + 1) * 5632], in_=wdt2_b[dt_]))
                        dgd_done[(tc, dt_)] = (Dd0 if slot == 0 else Dd1, (Dd0 if slot == 0 else Dd1).n)

                        @block.tensor
                        def _(t, tc=tc, dt_=dt_, slot=slot):
                            t.wait_ge(dgd_done[(tc, dt_)][0].h, dgd_done[(tc, dt_)][1])
                            if dt_ == 0:
                                V.wait(t)
                            if tc * 16 + dt_ >= 2:
                                t.wait_ge(Ac.h, down_ac[tc * 16 + dt_ - 2])
                            elif dt_ == 0:
                                Ac.wait(t)
                            ps = pS1 if dt_ % 2 == 0 else pS2
                            for k in range(44):
                                mm = t.matmul(ps[:, 0:512],
                                              s_wd[:, slot * 5632 + k * 128: slot * 5632 + (k + 1) * 128],
                                              s_hdn[:, k * 512:(k + 1) * 512],
                                              start=(k == 0), stop=(k == 43))
                            P.inc(mm)
                        dslot_free[slot] = P.n

                        @SCA
                        def _(sc, dt_=dt_):
                            P.wait(sc)
                            Pk.wait(sc)
                            ps = pS1 if dt_ % 2 == 0 else pS2
                            sc.activation(s_epg[:], ps[:, 0:512], AF.Silu,
                                                 scale=s_ds[:, dt_:dt_ + 1])
                            Ac.inc(sc.nop(nofuse=True))
                        down_ac[tc * 16 + dt_] = Ac.n

                        @block.sync
                        def _(s, dt_=dt_, tc=tc):
                            Ac.wait(s)
                            Pk.dinc(s.dma_start(out=g_d[dt_ * 128:(dt_ + 1) * 128,
                                                        tc * 512:(tc + 1) * 512],
                                                in_=s_epg[:]))

            barrier()

        barrier()

        with ExitStack() as scB:
            sbB = mk(scB)
            s_yfm = sbB("s_yfm", [128, 16384], BF16)

            with ExitStack() as scy:
                sby = mk(scy)
                s_xtc = sby("s_xtc", [128, 2 * 2048], BF16)
                s_zc = sby("s_zc", [128, 2 * 2048], BF16)
                s_sbc = sby("s_sbc", [128, 2 * 2048], BF16)
                s_bfm = sby("s_bfm2", [128, 4096], BF16)
                s_cfm = sby("s_cfm2", [128, 4096], BF16)
                s_rows = sby("s_rows", [2, 2 * 1024], BF16)
                s_E = sby("s_E", [128, 2 * 1024], BF16)
                s_G = sby("s_G", [128, 2 * 1024], BF16)
                s_ysc = sby("s_ysc", [128, 512])
                s_ytm = sby("s_ytm", [128, 2048], BF16)
                s_xD = sby("s_xD", [128, 2048], BF16)
                s_ynb = sby("s_ynb", [128, 2048], BF16)
                s_sqd = sby("s_sqd", [128, 512])
                s_ssq = sby("s_ssq", [128, 4])
                s_rg = sby("s_rg", [128, 4])
                s_wo = sby("s_wo", [128, 2 * 2048], BF16)
                s_ep1 = sby("s_ep1", [128, 512])
                s_ep2 = sby("s_ep2", [128, 512])
                s_eps1 = sby("s_eps1", [128, 512])
                s_epg = sby("s_epg2", [128, 512], BF16)
                s_epx = sby("s_epx", [128, 512])
                s_out = sby("s_out", [128, 2 * 512])
                pY = es.enter_context(nc.psum_tensor("pY", [128, 512], F32))

                oslot_free = [0, 0]
                ogd_done = {}
                r_args = {}
                r_E = {}
                r_G = {}
                r_G0 = {}
                r_Ymm = {}
                r_EV = {}
                for hp in range(2):
                    if hp == 1:
                        barrier()

                    @block.sync
                    def _(s, hp=hp):
                        V.wait(s)
                        Ac.wait(s)
                        Lrld.dinc(s.dma_start(out=s_bfm[:], in_=b_park[hp]))
                        Lrld.dinc(s.dma_start(out=s_cfm[:], in_=c_park[hp]))
                        for pc in range(2):
                            Lsp = Ls0 if pc % 2 == 0 else Ls1
                            Lsp.dinc(s.dma_start(out=ap3(s_sbc, pc * 2048, (2048, 1), (1, 2048)),
                                                 in_=ssb_d[hp * NCH + pc]))
                            Lsp.dinc(s.dma_start(out=ap3(s_xtc, pc * 2048, (2048, 1), (1, 2048)),
                                                 in_=xtm_park[hp][:, pc * 2048:(pc + 1) * 2048]))
                            Lsp.dinc(s.dma_start(out=ap3(s_zc, pc * 2048, (2048, 1), (1, 2048)),
                                                 in_=z_park[hp][:, pc * 2048:(pc + 1) * 2048]))
                    hp_rld = Lrld.n

                    def emit_front(c, g):
                        cs = (c % 2) * 2048
                        gp = g % 2
                        h0 = hp * 32 + g * 8

                        @block.sync
                        def _(s, c=c, g=g, gp=gp, h0=h0):
                            key = (hp, c, g - 2) if g >= 2 else (hp, c - 1, g + 2)
                            if key in r_args:
                                s.wait_ge(P.h, r_args[key])
                            Lrs = Lr if gp == 0 else Lr1
                            Lrs.dinc(s.dma_start(
                                out=s_rows[0:1, gp * 1024:(gp + 1) * 1024].rearrange(
                                    "o (a b) -> o a b", a=8),
                                in_=acs_hi_d[h0:h0 + 8, c * 128:(c + 1) * 128].rearrange(
                                    "(o a) b -> o a b", o=1)))
                            Lrs.dinc(s.dma_start(
                                out=s_rows[1:2, gp * 1024:(gp + 1) * 1024].rearrange(
                                    "o (a b) -> o a b", a=8),
                                in_=acs_lo_d[h0:h0 + 8, c * 128:(c + 1) * 128].rearrange(
                                    "(o a) b -> o a b", o=1)))
                        lr_sem = Lr if gp == 0 else Lr1
                        lr_n = lr_sem.n

                        @block.tensor
                        def _(t, c=c, g=g, gp=gp, h0=h0, lr_n=lr_n, lr_sem=lr_sem):
                            t.wait_ge(lr_sem.h, lr_n)
                            key = (hp, c, g - 2) if g >= 2 else (hp, c - 1, g + 2)
                            if key in r_E:
                                t.wait_ge(Ac.h, r_E[key])
                            for nchk in range(2):
                                rr = s_rep2[:, (h0 + nchk * 4) * 128: (h0 + nchk * 4 + 4) * 128]
                                t.matmul(pBIG[:, gp * 1024 + nchk * 512: gp * 1024 + (nchk + 1) * 512],
                                         s_vst[:, c * 128:(c + 1) * 128], rr,
                                         start=True, stop=False)
                                t.matmul(pBIG[:, gp * 1024 + nchk * 512: gp * 1024 + (nchk + 1) * 512],
                                         s_ones2[:],
                                         s_rows[:, gp * 1024 + nchk * 512: gp * 1024 + (nchk + 1) * 512],
                                         start=False, stop=False)
                                mm = t.matmul(pBIG[:, gp * 1024 + nchk * 512: gp * 1024 + (nchk + 1) * 512],
                                              s_ngt[:], s_idx4[:],
                                              start=False, stop=True)
                            P.inc(mm)
                        r_args[(hp, c, g)] = P.n

                        @SCA
                        def _(sc, c=c, g=g, gp=gp):
                            sc.wait_ge(P.h, r_args[(hp, c, g)])
                            key = (hp, c, g - 2) if g >= 2 else (hp, c - 1, g + 2)
                            if key in r_G:
                                sc.wait_ge(V.h, r_G[key])
                            sc.activation(s_E[:, gp * 1024:(gp + 1) * 1024],
                                          pBIG[:, gp * 1024:(gp + 1) * 1024], AF.Exp)
                            Ac.inc(sc.nop(nofuse=True))
                        r_E[(hp, c, g)] = Ac.n

                        @block.tensor
                        def _(t, c=c, g=g):
                            t.wait_ge(Lrld.h, hp_rld)
                            key = (hp, c, g - 1) if g >= 1 else (hp, c - 1, 3)
                            if key in r_G:
                                t.wait_ge(V.h, r_G[key])
                            mm = t.matmul(pS1[:, 0:128],
                                          s_bfm[:, g * 1024 + c * 128: g * 1024 + (c + 1) * 128],
                                          s_cfm[:, g * 1024 + c * 128: g * 1024 + (c + 1) * 128],
                                          start=True, stop=True)
                            P.inc(mm)
                        r_G0[(hp, c, g)] = P.n

                        @VEC
                        def _(v, c=c, g=g, gp=gp):
                            v.wait_ge(P.h, r_G0[(hp, c, g)])
                            v.wait_ge(Ac.h, r_E[(hp, c, g)])
                            g0rep = bass.AP(tensor=pS1, offset=0,
                                            ap=[[pS1.ap().ap[0][0], 128], [0, 8], [1, 128]])
                            v.tensor_tensor(
                                s_G.ap()[:, gp * 1024:(gp + 1) * 1024].rearrange(
                                    "p (h l) -> p h l", h=8),
                                g0rep,
                                s_E.ap()[:, gp * 1024:(gp + 1) * 1024].rearrange(
                                    "p (h l) -> p h l", h=8),
                                ALU.mult)
                            V.inc(v.nop(nofuse=True))
                        r_G[(hp, c, g)] = V.n

                    def emit_back(c, g):
                        cs = (c % 2) * 2048
                        gp = g % 2

                        @block.tensor
                        def _(t, c=c, g=g, gp=gp, cs=cs):
                            t.wait_ge(V.h, r_G[(hp, c, g)])
                            key = (hp, c, g - 1) if g >= 1 else (hp, c - 1, 3)
                            if key in r_EV:
                                t.wait_ge(V.h, r_EV[key])
                            (Ls0 if c % 2 == 0 else Ls1).wait(t)
                            for hh in range(8):
                                hloc = g * 8 + hh
                                t.matmul(pS2[:, hh * 64:(hh + 1) * 64],
                                         s_G[:, gp * 1024 + hh * 128: gp * 1024 + (hh + 1) * 128],
                                         s_xtc[:, cs + hloc * 64: cs + (hloc + 1) * 64],
                                         start=True, stop=True)
                            for hh in range(8):
                                hloc = g * 8 + hh
                                mm = t.matmul(pY[:, hh * 64:(hh + 1) * 64],
                                              s_cfm[:, g * 1024 + c * 128: g * 1024 + (c + 1) * 128],
                                              s_sbc[:, cs + hloc * 64: cs + (hloc + 1) * 64],
                                              start=True, stop=True)
                            P.inc(mm)
                        r_Ymm[(hp, c, g)] = P.n

                        @VEC
                        def _(v, c=c, g=g):
                            v.wait_ge(P.h, r_Ymm[(hp, c, g)])
                            ea = ap3(s_expa, c * 64 + hp * 32 + g * 8, (1, 8), (0, 64))
                            v.tensor_tensor(
                                s_ysc.ap().rearrange("p (h q) -> p h q", h=8),
                                bass.AP(tensor=pY, offset=0,
                                        ap=[[pY.ap().ap[0][0], 128], [64, 8], [1, 64]]),
                                ea, ALU.mult)
                            v.scalar_tensor_tensor(
                                s_ytm[:, g * 512:(g + 1) * 512], s_ysc[:], 1.0,
                                pS2[:, 0:512], ALU.bypass, ALU.add)
                            V.inc(v.nop(nofuse=True))
                        r_EV[(hp, c, g)] = V.n

                    def emit_tail(c):
                        cs = (c % 2) * 2048

                        @VEC
                        def _(v, c=c, cs=cs):
                            dap = ap3(s_db16, hp * 32, (1, 32), (0, 64))
                            v.tensor_tensor(
                                s_xD.ap().rearrange("p (h q) -> p h q", h=32),
                                ap3(s_xtc, cs, (64, 32), (1, 64)),
                                dap, ALU.mult)
                            v.tensor_add(s_ytm[:], s_ytm[:], s_xD[:])
                            v.tensor_mul(s_ytm[:], s_ytm[:], s_zc[:, cs:cs + 2048])
                            V.inc(v.nop(nofuse=True))
                        r_tail = V.n

                        @block.sync
                        def _(s, c=c, cs=cs, r_tail=r_tail):
                            s.wait_ge(V.h, r_tail)
                            if c + 2 < NCH:
                                Lsp = Ls0 if c % 2 == 0 else Ls1
                                Lsp.dinc(s.dma_start(out=ap3(s_sbc, cs, (2048, 1), (1, 2048)),
                                                     in_=ssb_d[hp * NCH + c + 2]))
                                Lsp.dinc(s.dma_start(out=ap3(s_xtc, cs, (2048, 1), (1, 2048)),
                                                     in_=xtm_park[hp][:, (c + 2) * 2048:(c + 3) * 2048]))
                                Lsp.dinc(s.dma_start(out=ap3(s_zc, cs, (2048, 1), (1, 2048)),
                                                     in_=z_park[hp][:, (c + 2) * 2048:(c + 3) * 2048]))

                        @SCA
                        def _(sc, c=c, r_tail=r_tail):
                            sc.wait_ge(V.h, r_tail)
                            for gl in range(4):
                                sc.activation(s_sqd[:], s_ytm[:, gl * 512:(gl + 1) * 512],
                                              AF.Square, accum_out=s_ssq[:, gl:gl + 1])
                            Ac.inc(sc.nop(nofuse=True))

                        @VEC
                        def _(v):
                            Ac.wait(v)
                            v.tensor_scalar_mul(s_rg[:], s_ssq[:], 1.0 / 512.0)
                            V.inc(v.nop(nofuse=True))

                        @SCA
                        def _(sc):
                            V.wait(sc)
                            sc.activation(s_rg[:], s_rg[:], AF.Ln, bias=s_epsc[:, 0:1])
                            sc.activation(s_rg[:], s_rg[:], AF.Exp, scale=-0.5)
                            Ac.inc(sc.nop(nofuse=True))

                        @VEC
                        def _(v):
                            Ac.wait(v)
                            rap = ap3(s_rg, 0, (1, 4), (0, 512))
                            v.tensor_tensor(
                                s_ynb.ap().rearrange("p (g q) -> p g q", g=4),
                                s_ytm.ap().rearrange("p (g q) -> p g q", g=4), rap, ALU.mult)
                            V.inc(v.nop(nofuse=True))

                        for grp in range(4):
                            @block.tensor
                            def _(t, grp=grp):
                                V.wait(t)
                                for j in range(4):
                                    i = grp * 4 + j
                                    mm = t.transpose(pT[:, j * 128:(j + 1) * 128],
                                                     s_ynb[:, i * 128:(i + 1) * 128], s_idb[:])
                                P.inc(mm)

                            @VEC
                            def _(v, grp=grp, c=c):
                                P.wait(v)
                                nwap = ap3(s_nwb, hp * 16 + grp * 4, (1, 4), (0, 128))
                                dst = ap3(s_yfm, (grp * 4) * 1024 + c * 128, (1024, 4), (1, 128))
                                v.tensor_tensor(dst,
                                                pT.ap()[:, 0:512].rearrange("p (j c) -> p j c", j=4),
                                                nwap, ALU.mult)
                                V.inc(v.nop(nofuse=True))

                    stages = [(c, g) for c in range(NCH) for g in range(4)]
                    for si in range(len(stages) + 1):
                        if si < len(stages):
                            emit_front(*stages[si])
                        if si >= 1:
                            cb, gb = stages[si - 1]
                            emit_back(cb, gb)
                            if gb == 3:
                                emit_tail(cb)

                    if debug and hp == 1:
                        @block.sync
                        def _(s):
                            V.wait(s)
                            St.dinc(s.dma_start(out=dbg_yfm, in_=s_yfm[:]))

                    wo_ev = {}
                    for dt_ in range(16):
                        slot = dt_ % 2

                        @block.scalar
                        def _(sc, dt_=dt_, hp=hp, slot=slot):
                            sc.wait_ge(cv_done[("wot", hp * 16 + dt_)][0].h,
                                       cv_done[("wot", hp * 16 + dt_)][1])
                            sc.wait_ge(P.h, oslot_free[slot])
                            (Gd0 if slot == 0 else Gd1).dinc(sc.dma_start(
                                out=s_wo[:, slot * 2048:(slot + 1) * 2048],
                                in_=wot_b[hp * 16 + dt_]))
                        ogd_done[(hp, dt_)] = (Gd0 if slot == 0 else Gd1,
                                               (Gd0 if slot == 0 else Gd1).n)

                        @block.tensor
                        def _(t, dt_=dt_, hp=hp, slot=slot):
                            t.wait_ge(ogd_done[(hp, dt_)][0].h, ogd_done[(hp, dt_)][1])
                            if dt_ == 0:
                                V.wait(t)
                            j = hp * 16 + dt_
                            if j >= 2 and (hp, dt_ - 2) in wo_ev:
                                t.wait_ge(V.h, wo_ev[(hp, dt_ - 2)])
                            elif hp == 1 and dt_ < 2:
                                V.wait(t)
                            pa = pS1[:, 0:512] if dt_ % 2 == 0 else pY[:, 0:512]
                            pb = pS2[:, 0:512] if dt_ % 2 == 0 else pBIG[:, 0:512]
                            for k in range(KD):
                                t.matmul(pa,
                                         s_wo[:, slot * 2048 + k * 128: slot * 2048 + (k + 1) * 128],
                                         s_yfm[:, k * 1024 + 0 * 512: k * 1024 + 1 * 512],
                                         start=(k == 0), stop=(k == KD - 1))
                                mm = t.matmul(pb,
                                              s_wo[:, slot * 2048 + k * 128: slot * 2048 + (k + 1) * 128],
                                              s_yfm[:, k * 1024 + 1 * 512: k * 1024 + 2 * 512],
                                              start=(k == 0), stop=(k == KD - 1))
                            P.inc(mm)
                        oslot_free[slot] = P.n
                        r_wo = P.n

                        for tc in range(2):
                            if hp == 0:
                                @VEC
                                def _(v, dt_=dt_, tc=tc, r_wo=r_wo):
                                    v.wait_ge(P.h, r_wo)
                                    Pk.wait(v)
                                    ps = (pS1[:, 0:512] if dt_ % 2 == 0 else pY[:, 0:512]) if tc == 0 \
                                        else (pS2[:, 0:512] if dt_ % 2 == 0 else pBIG[:, 0:512])
                                    v.tensor_copy(s_ep1[:], ps)
                                    V.inc(v.nop(nofuse=True))

                                @block.sync
                                def _(s, dt_=dt_, tc=tc):
                                    V.wait(s)
                                    Pk.dinc(s.dma_start(out=s1_d[dt_ * 128:(dt_ + 1) * 128,
                                                                 tc * 512:(tc + 1) * 512],
                                                        in_=s_ep1[:]))
                            else:
                                @block.sync
                                def _(s, dt_=dt_, tc=tc):
                                    V.wait(s)
                                    Pk.wait(s)
                                    Lep.dinc(s.dma_start(out=s_eps1[:],
                                                       in_=s1_d[dt_ * 128:(dt_ + 1) * 128,
                                                                tc * 512:(tc + 1) * 512]))
                                    Lep.dinc(s.dma_start(out=s_epg[:],
                                                       in_=g_d[dt_ * 128:(dt_ + 1) * 128,
                                                               tc * 512:(tc + 1) * 512]))
                                    Lep.dinc(s.dma_start(out=s_epx[:],
                                                       in_=xfmT[dt_ * 128:(dt_ + 1) * 128,
                                                                tc * 512:(tc + 1) * 512]))

                                @VEC
                                def _(v, dt_=dt_, tc=tc, r_wo=r_wo):
                                    v.wait_ge(P.h, r_wo)
                                    Lep.wait(v)
                                    St.wait(v)
                                    ps = (pS1[:, 0:512] if dt_ % 2 == 0 else pY[:, 0:512]) if tc == 0 \
                                        else (pS2[:, 0:512] if dt_ % 2 == 0 else pBIG[:, 0:512])
                                    o = ((dt_ * 2 + tc) % 2) * 512
                                    v.scalar_tensor_tensor(s_ep1[:], s_eps1[:], 1.0, ps,
                                                           ALU.bypass, ALU.add)
                                    v.tensor_mul(s_ep2[:], s_ep1[:], s_epg[:])
                                    v.tensor_add(s_out[:, o:o + 512], s_ep2[:], s_epx[:])
                                    V.inc(v.nop(nofuse=True))

                                @block.sync
                                def _(s, dt_=dt_, tc=tc):
                                    V.wait(s)
                                    o = ((dt_ * 2 + tc) % 2) * 512
                                    St.dinc(s.dma_start(
                                        out=outT[dt_ * 128:(dt_ + 1) * 128, tc * 512:(tc + 1) * 512],
                                        in_=s_out[:, o:o + 512]))
                        wo_ev[(hp, dt_)] = V.n

        @block.sync
        def _(s):
            St.wait(s)
            Pk.wait(s)

    return nc


def _prep_consts():
    idx = np.arange(128)
    tri = (idx[:, None] <= idx[None, :]).astype(np.float32)
    triu = (idx[:, None] > idx[None, :]).astype(np.float32)
    ident = np.eye(128, dtype=np.float32)
    identb = ident.astype(ml_dtypes.bfloat16)
    negtriT = np.where(idx[None, :] > idx[:, None], np.float32(NEG),
                       np.float32(0.0)).astype(ml_dtypes.bfloat16)
    identx4 = np.tile(np.eye(128, dtype=np.float32), (1, 4)).astype(ml_dtypes.bfloat16)
    k = np.arange(128)
    hl = np.arange(8192)
    rep2 = ((k[:, None] % 64) == (hl[None, :] // 128)).astype(np.float32).astype(ml_dtypes.bfloat16)
    ones2 = np.ones((2, 128), np.float32).astype(ml_dtypes.bfloat16)
    return dict(trif=tri, triuf=triu, identf=ident, identb=identb, negtriT=negtriT,
                identx4=identx4, rep2=rep2, ones2=ones2,
                onescol=np.ones((128, 1), np.float32),
                onesrow=np.ones((1, 128), np.float32),
                zeros8=np.zeros((128, 8), np.float32),
                epscol=np.full((128, 1), EPS, np.float32))


def make_in_maps(inputs):
    x = np.asarray(inputs["L0_attn_hidden_states"], np.float32)
    W_in = np.asarray(inputs["L0_attn_W_in"], np.float32)
    conv_w = np.asarray(inputs["L0_attn_conv_w"], np.float32)
    conv_b = np.asarray(inputs["L0_attn_conv_b"], np.float32)
    A = np.asarray(inputs["L0_attn_A"], np.float32)
    D = np.asarray(inputs["L0_attn_D"], np.float32)
    dtb = np.asarray(inputs["L0_attn_dt_bias"], np.float32)
    nw = np.asarray(inputs["L0_attn_norm_weight"], np.float32)
    W_out = np.asarray(inputs["L0_attn_W_out"], np.float32)
    gw = np.asarray(inputs["L0_ffn_gate_w"], np.float32)
    uw = np.asarray(inputs["L0_ffn_up_w"], np.float32)
    dw = np.asarray(inputs["L0_ffn_down_w"], np.float32)
    gs = np.asarray(inputs["L0_ffn_gate_s"], np.float32)
    us = np.asarray(inputs["L0_ffn_up_s"], np.float32)
    ds = np.asarray(inputs["L0_ffn_down_s"], np.float32)

    winT = W_in.T
    def tile_cols(wt, c0, c1, kk, w):
        blk = wt[:, c0:c1]
        nt = (c1 - c0) // w
        out = np.empty((nt, 128, kk * w), np.float32)
        for i in range(nt):
            out[i] = blk[:, i * w:(i + 1) * w].reshape(kk, 128, w).transpose(1, 0, 2).reshape(128, kk * w)
        return np.ascontiguousarray(out)

    wxt_np = np.concatenate([tile_cols(winT, 4096, 8192, 16, 128),
                             tile_cols(winT, 8192, 9216, 16, 128),
                             tile_cols(winT, 9216, 10240, 16, 128)], 0)
    idx = []
    for hp in range(2):
        idx += [16 * hp + i for i in range(16)]
        idx += [32 + 4 * hp + i for i in range(4)]
        idx += [40 + 4 * hp + i for i in range(4)]
    wzt_np = tile_cols(winT, 0, 4096, 16, 512)
    wdtt_np = tile_cols(winT, 10240, 10304, 16, 64)[0]
    wgt_np = tile_cols(gw.T, 0, 5632, 16, 128)
    wut_np = tile_cols(uw.T, 0, 5632, 16, 128)
    wdt2_np = tile_cols(dw.T, 0, 2048, 44, 128)
    woT = W_out.T
    wot_np = np.empty((32, 128, 2048), np.float32)
    for hp in range(2):
        for d in range(16):
            blk = woT[2048 * hp:2048 * (hp + 1), d * 128:(d + 1) * 128]
            wot_np[hp * 16 + d] = blk.reshape(16, 128, 128).transpose(1, 0, 2).reshape(128, 2048)

    shared = dict(
        wxt=wxt_np, wzt=wzt_np, wdtt=wdtt_np, wgt=wgt_np, wut=wut_np,
        wdt2=wdt2_np, wot=wot_np,
        convw=np.ascontiguousarray(conv_w.reshape(48, 128, 4).transpose(1, 0, 2).reshape(128, 192)),
        convb=np.ascontiguousarray(conv_b.reshape(48, 128).T),
        hrows=np.concatenate([dtb, A, D]).reshape(1, 192).astype(np.float32),
        nwcol=np.ascontiguousarray(nw.reshape(32, 128).T),
        gscol=np.ascontiguousarray(gs.reshape(44, 128).T),
        uscol=np.ascontiguousarray(us.reshape(44, 128).T),
        dscol=np.ascontiguousarray(ds.reshape(16, 128).T),
        **_prep_consts(),
    )
    maps = []
    for c in range(8):
        b, h = c // 2, c % 2
        t0 = h * 1024
        halo = x[b, t0 - 3:t0] if h == 1 else np.zeros((3, D_MODEL), np.float32)
        alph = np.zeros((128, 4), np.float32)
        if h == 1:
            alph[:, b] = 1.0
        m = dict(shared)
        m["xtm"] = np.ascontiguousarray(np.concatenate([halo, x[b, t0:t0 + 1024]], 0))
        m["xfmT"] = np.ascontiguousarray(x[b, t0:t0 + 1024].T)
        m["alphas"] = alph
        maps.append(m)
    return maps


def kernel(**inputs):
    maps = make_in_maps(inputs)
    nc = build_nc(debug=False)
    res = run_bass_kernel_spmd(nc, maps, core_ids=list(range(8)))
    out = np.zeros((4, 2048, 2048), np.float32)
    for c in range(8):
        b, h = c // 2, c % 2
        out[b, h * 1024:(h + 1) * 1024, :] = res.results[c]["outT"].T
    return out
